# version 11
# baseline (speedup 1.0000x reference)
import numpy as np
import concourse.bass as bass
import concourse.mybir as mybir
from concourse.bass_utils import run_bass_kernel_spmd

F32 = mybir.dt.float32
BF16 = mybir.dt.bfloat16
I32 = mybir.dt.int32
AF = mybir.ActivationFunctionType
ALU = mybir.AluOpType
AX = mybir.AxisListType

B, L, DM, DI, NST, DTR, NE, IFF = 2, 8192, 1024, 2048, 16, 64, 96, 2816
DS = 512
TOK = 2048
LB = 8192
L1C = 512
N1C = LB // L1C
LC = 512
NCH = LB // LC
NQ = 4
NGRP = NST // NQ
GROUPS = [[0, 1, 2, 3], [4, 5, 6, 7]]
WGROUP = [list(range(8))]
IFS = IFF // 8
WDS = DM // 8
MAGIC1 = 12582912.0


ENGINES = ("sync", "vector", "scalar", "tensor", "gpsimd")


class Op:
    __slots__ = ("eng", "fn", "deps", "soft", "kind", "idx", "sig", "val", "name", "sem", "stream")

    def __init__(self, eng, fn, deps, soft, kind, name):
        self.eng = eng
        self.fn = fn
        self.deps = deps
        self.soft = soft
        self.kind = kind
        self.idx = -1
        self.sig = False
        self.val = 0
        self.name = name


class Prog:
    def __init__(self):
        self.ops = {e: [] for e in ENGINES}
        self.all = []

    def op(self, eng, fn, deps=(), soft=(), kind="c", name="", stream=None):
        o = Op(eng, fn, list(deps), set(id(s) for s in soft), kind, name)
        o.idx = len(self.ops[eng])
        o.stream = stream
        self.ops[eng].append(o)
        self.all.append(o)
        return o

    def dma(self, eng, fn, deps=(), name="", stream=None):
        if stream is None:
            stream = "".join(ch for ch in name if not ch.isdigit())
        return self.op(eng, fn, deps, (), "d", name, stream=stream)

    def cc(self, fn, deps=(), name=""):
        return self.op("gpsimd", fn, deps, (), "cc", name)

    def emit(self, nc, bank, final=()):
        if final:
            self.op("sync", lambda e: e.nop(), list(final), (), "c", "terminal")
        for o in self.all:
            for d in o.deps:
                if d.kind == "c" and d.eng == o.eng and id(d) in o.soft:
                    continue
                if hasattr(d, "sem"):
                    if d.kind == "c" and not d.sig:
                        raise RuntimeError(
                            f"dep {d.name} of {o.name} crosses blocks unsignaled")
                    continue
                d.sig = True
        sems, cnt = bank.sems, bank.cnt
        for e in ENGINES:
            for o in self.ops[e]:
                if o.kind == "cc":
                    o.sem = nc.alloc_semaphore(f"scc{bank.ncc}")
                    bank.ncc += 1
                    o.sig = True
                    o.val = 1
                elif o.kind == "d":
                    o.sig = True
                    st = bank.stream(o.stream)
                    if st[2] is not None:
                        o.deps.append(st[2])
                    st[1] += 16
                    o.sem = st[0]
                    o.val = st[1]
                    st[2] = o
                else:
                    if o.sig:
                        cnt[("c", e)] += 1
                    o.sem = sems[("c", e)]
                    o.val = cnt[("c", e)]

        def emit_stream(eng_obj, e):
            waited = {}
            for o in self.ops[e]:
                for d in o.deps:
                    if d.kind == "c" and d.eng == e and id(d) in o.soft:
                        continue
                    if d.kind == "c" and d.eng == e and not d.sig:
                        raise RuntimeError(f"unsignaled dep {d.name} of {o.name}")
                    key = id(d.sem)
                    if waited.get(key, -1) >= d.val:
                        continue
                    eng_obj.wait_ge(d.sem, d.val)
                    waited[key] = d.val
                ins = o.fn(eng_obj)
                if o.kind == "cc":
                    ins.then_inc(o.sem, 1)
                elif o.kind == "d":
                    ins.then_inc(o.sem, 16)
                elif o.sig:
                    if ins is None:
                        raise RuntimeError(f"op {o.name} returned no instruction")
                    ins.then_inc(o.sem, 1)
            return None

        with nc.Block() as blk:
            blk.sync(lambda e: emit_stream(e, "sync"))
            blk.vector(lambda e: emit_stream(e, "vector"))
            blk.scalar(lambda e: emit_stream(e, "scalar"))
            blk.tensor(lambda e: emit_stream(e, "tensor"))
            blk.gpsimd(lambda e: emit_stream(e, "gpsimd"))


class SemBank:
    def __init__(self, nc):
        self.nc = nc
        self.sems = {}
        for e in ENGINES:
            self.sems[("c", e)] = nc.alloc_semaphore(f"sc_{e}")
        self.cnt = {k: 0 for k in self.sems}
        self.streams = {}
        self.ncc = 0

    def stream(self, name):
        if name not in self.streams:
            self.streams[name] = [self.nc.alloc_semaphore(f"sm_{len(self.streams)}"), 0, None]
        return self.streams[name]


def quant_ops(P, v, x_f32, out_bf16, scr):
    p, Fv = x_f32.shape
    nb = Fv // 16
    mx, q1, q2, q3 = scr["mx"], scr["q1"], scr["q2"], scr["q3"]
    o = []

    def vop(fn, deps, name, soft=()):
        o.append(P.op("vector", fn, deps, soft, "c", name))
        return o[-1]

    t_mx = vop(lambda e: e.tensor_reduce(
        out=mx[:p, 0:nb], in_=x_f32.rearrange("p (a b) -> p a b", b=16),
        axis=AX.X, op=ALU.max, apply_absolute_value=True), [], "q.mx")
    t_mxc = vop(lambda e: e.tensor_scalar(
        out=mx[:p, 0:nb], in0=mx[:p, 0:nb], scalar1=6e-12, scalar2=None,
        op0=ALU.max), [t_mx], "q.mxc")
    t_r = vop(lambda e: e.reciprocal(mx[:p, nb:2 * nb], mx[:p, 0:nb]),
              [t_mxc], "q.r")
    t_r6 = vop(lambda e: e.tensor_scalar(
        out=mx[:p, nb:2 * nb], in0=mx[:p, nb:2 * nb], scalar1=6.0,
        scalar2=None, op0=ALU.mult), [t_r], "q.r6")
    t_s6 = vop(lambda e: e.tensor_scalar(
        out=mx[:p, 2 * nb:3 * nb], in0=mx[:p, 0:nb], scalar1=1.0 / 6.0,
        scalar2=None, op0=ALU.mult), [t_mxc], "q.s6")
    t_xn = vop(lambda e: e.tensor_tensor(
        out=q1[:p, 0:Fv], in0=x_f32,
        in1=mx[:p, nb:2 * nb].unsqueeze(2).broadcast_to([p, nb, 16]),
        op=ALU.mult), [t_r6], "q.xn")
    t_eb = vop(lambda e: e.tensor_scalar(
        out=q2[:p, 0:Fv].bitcast(I32), in0=q1[:p, 0:Fv].bitcast(I32),
        scalar1=0x7F800000, scalar2=None, op0=ALU.bitwise_and), [t_xn], "q.eb")
    t_ebc = vop(lambda e: e.tensor_scalar(
        out=q2[:p, 0:Fv].bitcast(I32), in0=q2[:p, 0:Fv].bitcast(I32),
        scalar1=0x3F800000, scalar2=0x40800000, op0=ALU.max, op1=ALU.min),
        [t_eb], "q.ebc")
    t_rs = vop(lambda e: e.tensor_scalar(
        out=q3[:p, 0:Fv].bitcast(I32), in0=q2[:p, 0:Fv].bitcast(I32),
        scalar1=0x7F800000, scalar2=None, op0=ALU.bitwise_xor),
        [t_ebc], "q.rstep")
    t_st = vop(lambda e: e.tensor_scalar(
        out=q2[:p, 0:Fv], in0=q2[:p, 0:Fv], scalar1=0.5, scalar2=None,
        op0=ALU.mult), [t_ebc], "q.step")
    t_t = vop(lambda e: e.tensor_tensor(
        out=q1[:p, 0:Fv], in0=q1[:p, 0:Fv], in1=q3[:p, 0:Fv], op=ALU.mult),
        [t_rs], "q.t")
    t_rr = vop(lambda e: e.tensor_scalar(
        out=q1[:p, 0:Fv], in0=q1[:p, 0:Fv], scalar1=MAGIC1, scalar2=MAGIC1,
        op0=ALU.add, op1=ALU.subtract), [t_t], "q.round")
    t_q = vop(lambda e: e.tensor_tensor(
        out=q1[:p, 0:Fv], in0=q1[:p, 0:Fv], in1=q2[:p, 0:Fv], op=ALU.mult),
        [t_rr, t_st], "q.q")
    t_out = vop(lambda e: e.tensor_tensor(
        out=out_bf16, in0=q1[:p, 0:Fv],
        in1=mx[:p, 2 * nb:3 * nb].unsqueeze(2).broadcast_to([p, nb, 16]),
        op=ALU.mult), [t_q, t_s6], "q.out")
    return t_mx, t_out


def ln_ops(P, x_f32, xn_out, stats, sq, name, consts=None, out_extra_dep=()):
    t_ms = P.op("vector", lambda e: e.tensor_reduce(
        out=stats[:, 0:1], in_=x_f32, axis=AX.X, op=ALU.add), [], (),
        "c", name + ".msum")
    t_m = P.op("vector", lambda e: e.tensor_scalar(
        out=stats[:, 0:1], in0=stats[:, 0:1], scalar1=1.0 / DM, scalar2=None,
        op0=ALU.mult), [t_ms], (), "c", name + ".m")
    t_sq = P.op("scalar", lambda e: e.activation(
        out=sq[:, :], in_=x_f32, func=AF.Square), [], (), "c", name + ".sq")
    t_ss = P.op("vector", lambda e: e.tensor_reduce(
        out=stats[:, 1:2], in_=sq[:, :], axis=AX.X, op=ALU.add), [t_sq], (),
        "c", name + ".ssum")
    t_s2 = P.op("vector", lambda e: e.tensor_scalar(
        out=stats[:, 1:2], in0=stats[:, 1:2], scalar1=1.0 / DM, scalar2=None,
        op0=ALU.mult), [t_ss], (), "c", name + ".ess")
    t_mm = P.op("vector", lambda e: e.tensor_tensor(
        out=stats[:, 2:3], in0=stats[:, 0:1], in1=stats[:, 0:1], op=ALU.mult),
        [t_m], (), "c", name + ".m2")
    t_var = P.op("vector", lambda e: e.tensor_tensor(
        out=stats[:, 1:2], in0=stats[:, 1:2], in1=stats[:, 2:3],
        op=ALU.subtract), [t_s2, t_mm], (), "c", name + ".var")
    t_ln = P.op("scalar", lambda e: e.activation(
        out=stats[:, 2:3], in_=stats[:, 1:2], func=AF.Ln, bias=consts["eps"]),
        [t_var] + consts["op"], (), "c", name + ".ln")
    t_rstd = P.op("scalar", lambda e: e.activation(
        out=stats[:, 2:3], in_=stats[:, 2:3], func=AF.Exp, scale=consts["mhalf"]),
        [t_ln], (), "c", name + ".rstd")
    t_xn = P.op("vector", lambda e: e.tensor_scalar(
        out=xn_out, in0=x_f32, scalar1=stats[:, 0:1], scalar2=stats[:, 2:3],
        op0=ALU.subtract, op1=ALU.mult),
        [t_rstd, t_m] + list(out_extra_dep), (), "c", name + ".xn")
    return [t_ms, t_sq], t_xn


def build(upto=2, dbg=(), skip_wq=False):
    nc = bass.Bass("TRN2", target_bir_lowering=False)
    dbg = set(dbg)
    bank = SemBank(nc)
    P = Prog()

    def inp(name, shape, dt=F32):
        return nc.dram_tensor(name, shape, dt, kind="ExternalInput").ap()

    x_t = inp("x_t", [TOK, DM])
    w_inT = inp("w_inT", [DM, 2 * DS])
    dconv = inp("dconv", [128, 16 * 128])
    bconv_t = inp("bconv_t", [128, 4])
    w_xpT = inp("w_xpT", [DS, NE])
    w_dtT = inp("w_dtT", [64, DS])
    bdt_t = inp("bdt_t", [128, 4])
    alog_t = inp("alog_t", [128, 4 * NST])
    d_t = inp("d_t", [128, 4])
    w_outT = inp("w_outT", [DS, DM])
    wg_sh = inp("wg_sh", [IFS, DM])
    wu_sh = inp("wu_sh", [IFS, DM])
    wd_sh = inp("wd_sh", [WDS, IFF])
    ident = inp("ident", [128, 128], BF16)
    identf = inp("identf", [128, 128], F32)

    o_y = nc.dram_tensor("o_y", [TOK, DM], F32, kind="ExternalOutput").ap()

    b_xnj = [nc.dram_tensor(f"b_xn{j}", [DM, 512], BF16) for j in range(4)]
    b_xn_gj = [nc.dram_tensor(f"b_xn_g{j}", [4 * DM, 512], BF16) for j in range(4)]
    b_xdp = nc.dram_tensor("b_xdp", [NE, LB], F32)
    b_xdf = nc.dram_tensor("b_xdf", [NE, LB], F32)
    b_u = nc.dram_tensor("b_u", [128, 4 * LB], F32)
    b_sres = nc.dram_tensor("b_sres", [128, 4 * LB], F32)
    b_wopj = [nc.dram_tensor(f"b_wop{j}", [4 * DM, 512], F32) for j in range(4)]
    b_rsj = [nc.dram_tensor(f"b_rs{j}", [DM, 512], F32) for j in range(4)]
    b_res = nc.dram_tensor("b_res", [TOK, DM], F32)
    b_hq = nc.dram_tensor("b_hq", [IFF, TOK], BF16)
    b_wqg = nc.dram_tensor("b_wqg", [DM, IFS], BF16)
    b_wqu = nc.dram_tensor("b_wqu", [DM, IFS], BF16)
    b_wqd = nc.dram_tensor("b_wqd", [IFF, WDS], BF16)
    b_wqg_g = nc.dram_tensor("b_wqg_g", [8 * DM, IFS], BF16)
    b_wqu_g = nc.dram_tensor("b_wqu_g", [8 * DM, IFS], BF16)
    b_wqd_g = nc.dram_tensor("b_wqd_g", [8 * IFF, WDS], BF16)

    taps = {}

    def tap(name, src_ap, shape, dt, deps):
        if name not in dbg:
            return
        t = nc.dram_tensor("tap_" + name, shape, dt, kind="ExternalOutput").ap()
        taps[name] = P.dma("sync", lambda e, s=src_ap, t=t: e.dma_start(out=t, in_=s),
                           deps, "tap." + name)

    sbt = nc.alloc_sbuf_tensor
    ARN_COLS = 51200
    arena = sbt("arena", [128, ARN_COLS], F32)
    arn_f = arena.ap()
    arn_b = arena.ap().bitcast(BF16)

    class Al:
        def __init__(self, start=0):
            self.c = start

        def f(self, cols):
            off = self.c
            self.c += cols
            assert self.c <= ARN_COLS, f"arena overflow {self.c}"
            return arn_f[:, off:off + cols]

        def b(self, cols):
            off = self.c
            self.c += (cols + 1) // 2
            assert self.c <= ARN_COLS, f"arena overflow {self.c}"
            return arn_b[:, 2 * off:2 * off + cols]

    s_id = sbt("s_id", [128, 128], BF16)
    s_idf = sbt("s_idf", [128, 128], F32)
    s_small = sbt("s_small", [128, 256], F32)
    negA = s_small.ap()[:, 0:64]
    bdt = s_small.ap()[:, 64:68]
    dcol = s_small.ap()[:, 68:72]
    carry = s_small.ap()[:, 72:136]
    stats = s_small.ap()[:, 140:160]
    c_eps = s_small.ap()[:, 160:161]
    c_mhalf = s_small.ap()[:, 161:162]
    cst0 = P.op("vector", lambda e: e.memset(c_eps, 1e-5), (), (), "c", "cst.eps")
    cst1 = P.op("vector", lambda e: e.memset(c_mhalf, -0.5), (), (), "c", "cst.mhalf")
    consts = {"eps": c_eps, "mhalf": c_mhalf, "op": [cst0, cst1]}

    p_a = nc.alloc_psum_tensor("p_a", [128, 512], F32)
    p_b = nc.alloc_psum_tensor("p_b", [128, 512], F32)
    p_c = nc.alloc_psum_tensor("p_c", [128, 512], F32)
    p_d = nc.alloc_psum_tensor("p_d", [128, 512], F32)
    p_e = nc.alloc_psum_tensor("p_e", [128, 1024], F32)
    p_t1 = nc.alloc_psum_tensor("p_t1", [128, 128], F32)
    p_t2 = nc.alloc_psum_tensor("p_t2", [128, 128], F32)
    pt = [p_t1.ap(), p_t2.ap()]
    ptb = [p_t1.ap().bitcast(BF16)[:, 0:128], p_t2.ap().bitcast(BF16)[:, 0:128]]

    xblock = {}
    ld_id = P.dma("sync", lambda e: e.dma_start(out=s_id.ap(), in_=ident), (), "ld.id")
    ld_idf = P.dma("sync", lambda e: e.dma_start(out=s_idf.ap(), in_=identf), (), "ld.idf")

    al = Al()
    s_w = al.f(IFF)
    s_wq = al.b(IFF)
    s_wt = al.b(22 * 128)
    scr = dict(mx=al.f(3 * (IFF // 16)), q1=al.f(IFF), q2=al.f(IFF), q3=al.f(IFF))
    wq_al_end = al.c

    wq_jobs = []
    if skip_wq:
        wq_jobs = []
    for r0 in range(0, IFS, 128) if not skip_wq else []:
        rows = min(128, IFS - r0)
        wq_jobs.append((wg_sh[r0:r0 + rows, :], rows, DM, b_wqg, r0))
        wq_jobs.append((wu_sh[r0:r0 + rows, :], rows, DM, b_wqu, r0))
    wq_jobs.append((wd_sh[0:WDS, :], WDS, IFF, b_wqd, 0))

    prev_store = None
    prev_tr = {}
    wq_stores = []
    for i, (src, rows, cols, dst, r0) in enumerate(wq_jobs):
        deps = [prev_store] if prev_store else []
        t_ld = P.dma("sync", lambda e, s=src, r=rows, c=cols:
                     e.dma_start(out=s_w[0:r, 0:c], in_=s), deps, f"wq.ld{i}")
        q0, q1o = quant_ops(P, None, s_w[0:rows, 0:cols], s_wq[0:rows, 0:cols], scr)
        q0.deps.append(t_ld)
        nk = cols // 128
        evs = []
        for kt in range(nk):
            par = kt % 2
            tdeps = [q1o, ld_id]
            if par in prev_tr:
                tdeps.append(prev_tr[par][1])
            t_tr = P.op("tensor", lambda e, k=kt, p=par, r=rows:
                        e.transpose(ptb[p][0:128, 0:r], s_wq[0:r, k * 128:(k + 1) * 128],
                                    s_id.ap()[0:r, 0:r]),
                        tdeps, (), "c", f"wq.tr{i}.{kt}")
            t_ev = P.op("vector", lambda e, k=kt, p=par, r=rows:
                        e.tensor_copy(s_wt[:, k * 128:k * 128 + r], ptb[p][0:128, 0:r]),
                        [t_tr], (), "c", f"wq.ev{i}.{kt}")
            prev_tr[par] = (t_tr, t_ev)
            xblock[("pt", par)] = t_ev
            evs.append(t_ev)
        t_st = P.dma("sync", lambda e, d=dst, r0_=r0, r=rows, n=nk:
                     e.dma_start(
                         out=d.ap()[0:n * 128, r0_:r0_ + r]
                             .rearrange("(k p) c -> p k c", p=128),
                         in_=s_wt.rearrange("p (k c) -> p k c", c=128)[:, 0:n, 0:r]),
                     evs, f"wq.st{i}")
        wq_stores.append(t_st)
        prev_store = t_st

    if skip_wq:
        class _F: pass
        cc_wg = cc_wu = cc_wd = None
    cc_wg = cc_wg if skip_wq else P.cc(lambda e: e.collective_compute(
        "AllGather", ALU.bypass, replica_groups=WGROUP,
        ins=[b_wqg.ap().opt()], outs=[b_wqg_g.ap().opt()]), wq_stores, "cc.wg")
    cc_wu = cc_wu if skip_wq else P.cc(lambda e: e.collective_compute(
        "AllGather", ALU.bypass, replica_groups=WGROUP,
        ins=[b_wqu.ap().opt()], outs=[b_wqu_g.ap().opt()]), wq_stores, "cc.wu")
    cc_wd = cc_wd if skip_wq else P.cc(lambda e: e.collective_compute(
        "AllGather", ALU.bypass, replica_groups=WGROUP,
        ins=[b_wqd.ap().opt()], outs=[b_wqd_g.ap().opt()]), wq_stores, "cc.wd")
    tap("wqg", b_wqg.ap(), [DM, IFS], BF16, [cc_wg])
    tap("wqd", b_wqd.ap(), [IFF, WDS], BF16, [cc_wd])

    al = Al(wq_al_end)
    s_x0 = [al.f(DM), al.f(DM)]
    s_sq = al.f(DM)
    s_xn = al.b(DM)
    s_xnT = al.b(8 * 128)
    ph0_end = al.c

    last_ln = None
    last_ev_x = {}
    xn_stores = []
    for tt in range(16):
        par = tt % 2
        deps = []
        if tt >= 2:
            deps.append(last_ln[tt - 2])
        t_ld = P.dma("sync", lambda e, t=tt, p=par:
                     e.dma_start(out=s_x0[p], in_=x_t[t * 128:(t + 1) * 128, :]),
                     deps, f"ph0.ldx{tt}", stream=f"ph0.ldx{tt % 2}p")
        f_ops, t_xn = ln_ops(P, s_x0[par], s_xn, stats, s_sq, f"ph0.ln{tt}", consts)
        for f in f_ops:
            f.deps.append(t_ld)
        if tt > 0:
            f_ops[0].deps.append(last_tr_done)
            f_ops[1].deps.append(last_ln[tt - 1])
        if last_ln is None:
            last_ln = {}
        last_ln[tt] = t_xn
        evs = []
        for kt in range(8):
            par2 = kt % 2
            tdeps = [t_xn, ld_id]
            if (tt, kt) != (0, 0) and par2 in last_ev_x:
                tdeps.append(last_ev_x[par2])
            t_tr = P.op("tensor", lambda e, k=kt, p=par2:
                        e.transpose(ptb[p], s_xn[:, k * 128:(k + 1) * 128], s_id.ap()),
                        tdeps, (), "c", f"ph0.tr{tt}.{kt}")
            evdeps = [t_tr]
            if tt > 0 and kt == 0:
                evdeps.append(xn_stores[tt - 1])
            t_ev = P.op("vector", lambda e, k=kt, p=par2:
                        e.tensor_copy(s_xnT[:, k * 128:(k + 1) * 128], ptb[p]),
                        evdeps, (), "c", f"ph0.ev{tt}.{kt}")
            last_ev_x[par2] = t_ev
            xblock[("pt", par2)] = t_ev
            evs.append(t_ev)
        last_tr_done = evs[-1]
        t_st = P.dma("sync", lambda e, t=tt:
                     e.dma_start(
                         out=b_xnj[t // 4].ap()[:, (t % 4) * 128:(t % 4 + 1) * 128]
                             .rearrange("(k p) c -> p k c", p=128),
                         in_=s_xnT.rearrange("p (k c) -> p k c", c=128)),
                     evs, f"ph0.st{tt}")
        xn_stores.append(t_st)

    cc_xns = []
    for j in range(4):
        cc_xns.append(P.cc(lambda e, j_=j: e.collective_compute(
            "AllGather", ALU.bypass, replica_groups=GROUPS,
            ins=[b_xnj[j_].ap().opt()], outs=[b_xn_gj[j_].ap().opt()]),
            xn_stores[4 * j:4 * j + 4], f"cc.xn{j}"))
    cc_xn = cc_xns[-1]
    if upto < 1:
        P.emit(nc, bank, final=list(taps.values()) + cc_xns + [cc_wg, cc_wu, cc_wd])
        return nc, taps

    al = Al(ph0_end)
    s_rhs = [al.b(8 * L1C), al.b(8 * L1C)]
    s_winT = al.b(8 * 1024)
    s_dconv = al.b(16 * 128)
    s_wxpT = al.b(4 * NE)
    s_xc = al.b(4 * (L1C + 3))
    s_u1 = al.f(4 * L1C)
    s_ub = al.b(4 * L1C)
    s_sr1 = al.f(4 * L1C)
    s_xdps = al.f(L1C)
    s_bconv = al.f(4)
    ph1_end = al.c
    assert ph1_end <= ARN_COLS

    ld_win = P.dma("gpsimd", lambda e: e.dma_start(
        out=s_winT.rearrange("p (k m) -> p k m", m=1024),
        in_=w_inT.rearrange("(k p) m -> p k m", p=128)),
        xn_stores, "ph1.ldwin")
    ld_dc = P.dma("gpsimd", lambda e: e.dma_start(out=s_dconv, in_=dconv),
                  xn_stores, "ph1.lddc")
    ld_wxp = P.dma("gpsimd", lambda e: e.dma_start(
        out=s_wxpT.rearrange("p (k m) -> p k m", m=NE),
        in_=w_xpT.rearrange("(k p) m -> p k m", p=128)),
        xn_stores, "ph1.ldwxp")
    ld_bc = P.dma("sync", lambda e: e.dma_start(out=s_bconv, in_=bconv_t),
                  xn_stores, "ph1.ldbc")
    t_hz = P.op("vector", lambda e: e.memset(s_xc, 0.0), xn_stores, (), "c", "ph1.hz")

    prev = dict()
    xdp_stores = []
    u_stores = []
    sr_stores = []
    for c1 in range(N1C):
        par = c1 % 2
        r, off = c1 // 4, (c1 % 4) * L1C
        deps = [cc_xns[c1 % 4]]
        if c1 >= 2:
            deps.append(prev[("mm_last", (c1 - 2) % 2)])
        t_ldr = P.dma("sync", lambda e, r_=r, j_=c1 % 4, p=par: e.dma_start(
            out=s_rhs[p].rearrange("p (k c) -> p k c", c=L1C),
            in_=b_xn_gj[j_].ap()[r_ * DM:(r_ + 1) * DM, :]
                .rearrange("(k p) c -> p k c", p=128)),
            deps, f"ph1.ldr{c1}", stream=f"ph1.ldr{c1 % 2}p")
        for m in range(8):
            pp = [p_a, p_b, p_c, p_d][m % 4]
            mdeps = [t_ldr, ld_win]
            key = ("winpsum", m % 4)
            if key in prev:
                mdeps.append(prev[key])
            mm = None
            for k in range(8):
                mm = P.op("tensor", lambda e, k_=k, m_=m, p_=pp, pr=par: e.matmul(
                    p_.ap()[:, 0:L1C],
                    s_winT[:, k_ * 1024 + m_ * 128:k_ * 1024 + (m_ + 1) * 128],
                    s_rhs[pr][:, k_ * L1C:(k_ + 1) * L1C],
                    start=(k_ == 0), stop=(k_ == 7)),
                    mdeps if k == 0 else [], (), "c", f"ph1.mm{c1}.{m}.{k}")
            if m == 7:
                prev[("mm_last", par)] = mm
            if m < 4:
                hdeps = []
                if c1 > 0:
                    hdeps = [prev[("xc_ev", m)]]
                    if ("conv_mm", m) in prev:
                        hdeps.append(prev[("conv_mm", m)])
                    t_h = P.op("vector", lambda e, m_=m: e.tensor_copy(
                        s_xc[:, m_ * 515:m_ * 515 + 3],
                        s_xc[:, m_ * 515 + 512:m_ * 515 + 515]),
                        hdeps, (), "c", f"ph1.halo{c1}.{m}")
                else:
                    t_h = t_hz
                t_ev = P.op("vector", lambda e, m_=m, p_=pp: e.tensor_copy(
                    s_xc[:, m_ * 515 + 3:m_ * 515 + 515], p_.ap()[:, 0:L1C]),
                    [mm, t_h], (), "c", f"ph1.xcev{c1}.{m}")
                prev[("xc_ev", m)] = t_ev
                prev[("winpsum", m % 4)] = t_ev
                prev[("xc_ready", m)] = t_ev
            else:
                mprev = m - 4
                sdeps = [mm]
                if ("sr_store",) in prev and c1 > 0 and mprev == 0:
                    sdeps.append(prev[("sr_store",)])
                t_sr = P.op("scalar", lambda e, m_=mprev, p_=pp: e.activation(
                    out=s_sr1[:, m_ * L1C:(m_ + 1) * L1C], in_=p_.ap()[:, 0:L1C],
                    func=AF.Silu), sdeps, (), "c", f"ph1.sr{c1}.{m}")
                prev[("winpsum", m % 4)] = t_sr
                prev[("sr", mprev)] = t_sr
        t_sst = P.dma("sync", lambda e, o=c1 * L1C: e.dma_start(
            out=b_sres.ap().rearrange("p (t c) -> p t c", t=4)[:, :, o:o + L1C],
            in_=s_sr1.rearrange("p (t c) -> p t c", t=4)),
            [prev[("sr", i)] for i in range(4)], f"ph1.srst{c1}")
        prev[("sr_store",)] = t_sst
        sr_stores.append(t_sst)
        for m in range(4):
            po_ = (m % 2) * 512
            cdeps = [prev[("xc_ready", m)], ld_dc]
            key = ("convpsum", m % 2)
            if key in prev:
                cdeps.append(prev[key])
            if m < 2 and ("xdp_ev",) in prev:
                cdeps.append(prev[("xdp_ev",)])
            mm = None
            for k in range(4):
                mm = P.op("tensor", lambda e, k_=k, m_=m, po=po_: e.matmul(
                    p_e.ap()[:, po:po + L1C],
                    s_dconv[:, (m_ * 4 + k_) * 128:(m_ * 4 + k_ + 1) * 128],
                    s_xc[:, m_ * 515 + k_:m_ * 515 + k_ + L1C],
                    start=(k_ == 0), stop=(k_ == 3)),
                    cdeps if k == 0 else [], (), "c", f"ph1.cv{c1}.{m}.{k}")
            prev[("conv_mm", m)] = mm
            udeps = [mm]
            if c1 > 0 and m == 0 and ("u_store",) in prev:
                udeps.append(prev[("u_store",)])
            t_us = P.op("scalar", lambda e, m_=m, po=po_: e.activation(
                out=s_u1[:, m_ * L1C:(m_ + 1) * L1C], in_=p_e.ap()[:, po:po + L1C],
                func=AF.Silu, bias=s_bconv[:, m_:m_ + 1]),
                udeps + [ld_bc], (), "c", f"ph1.us{c1}.{m}")
            ubdeps = [t_us]
            if ("xdbl_mm",) in prev:
                ubdeps.append(prev[("xdbl_mm",)])
            t_ub = P.op("vector", lambda e, m_=m: e.tensor_copy(
                s_ub[:, m_ * L1C:(m_ + 1) * L1C],
                s_u1[:, m_ * L1C:(m_ + 1) * L1C]),
                ubdeps, (), "c", f"ph1.ub{c1}.{m}")
            prev[("convpsum", m % 2)] = t_us
            prev[("ub", m)] = t_ub
            prev[("us", m)] = t_us
        t_ust = P.dma("sync", lambda e, o=c1 * L1C: e.dma_start(
            out=b_u.ap().rearrange("p (t c) -> p t c", t=4)[:, :, o:o + L1C],
            in_=s_u1.rearrange("p (t c) -> p t c", t=4)),
            [prev[("us", i)] for i in range(4)], f"ph1.ust{c1}")
        prev[("u_store",)] = t_ust
        u_stores.append(t_ust)
        xdeps = [prev[("ub", i)] for i in range(4)] + [ld_wxp]
        xdeps += [prev[("us", i)] for i in range(4)]
        if ("xdp_ev",) in prev:
            xdeps.append(prev[("xdp_ev",)])
        mm = None
        for k in range(4):
            mm = P.op("tensor", lambda e, k_=k: e.matmul(
                p_e.ap()[0:NE, 0:L1C],
                s_wxpT[:, k_ * NE:(k_ + 1) * NE],
                s_ub[:, k_ * L1C:(k_ + 1) * L1C],
                start=(k_ == 0), stop=(k_ == 3)),
                xdeps if k == 0 else [], (), "c", f"ph1.xd{c1}.{k}")
        prev[("xdbl_mm",)] = mm
        xst_dep = []
        if ("xdp_st",) in prev:
            xst_dep.append(prev[("xdp_st",)])
        t_xe = P.op("scalar", lambda e: e.activation(
            out=s_xdps[0:NE, :], in_=p_e.ap()[0:NE, 0:L1C], func=AF.Copy),
            [mm] + xst_dep, (), "c", f"ph1.xde{c1}")
        prev[("xdp_ev",)] = t_xe
        t_xst = P.dma("sync", lambda e, o=c1 * L1C: e.dma_start(
            out=b_xdp[0:NE, o:o + L1C], in_=s_xdps[0:NE, :]),
            [t_xe], f"ph1.xdst{c1}")
        prev[("xdp_st",)] = t_xst
        xdp_stores.append(t_xst)

    cc_xd = P.cc(lambda e: e.collective_compute(
        "AllReduce", ALU.add, replica_groups=GROUPS,
        ins=[b_xdp.ap().opt()], outs=[b_xdf.ap().opt()]), xdp_stores, "cc.xd")
    tap("xdf", b_xdf.ap(), [NE, LB], F32, [cc_xd])
    tap("u", b_u.ap(), [128, 4 * LB], F32, [u_stores[-1]] + u_stores[:-1])
    tap("sres", b_sres.ap(), [128, 4 * LB], F32, [sr_stores[-1]] + sr_stores[:-1])

    if upto < 2:
        P.emit(nc, bank, final=list(taps.values()) + [cc_xd] +
               ([] if skip_wq else [cc_wg, cc_wu, cc_wd]))
        return nc, taps

    P.emit(nc, bank)
    nc.all_engine_barrier()
    P2 = Prog()
    P = P2

    gate_deps = [cc_xd, u_stores[-1], sr_stores[-1], xdp_stores[-1],
                 xn_stores[-1]]
    if wq_stores:
        gate_deps += [wq_stores[-1]]
    ph2_gate = P.op("sync", lambda e: e.nop(), gate_deps, (), "c", "ph2.gate")

    al = Al()
    LCP = LC + 1
    sX = [al.f(4 * LCP), al.f(4 * LCP)]
    sY = [al.f(4 * LCP), al.f(4 * LCP)]
    sHC = [al.f(4 * LC), al.f(4 * LC)]
    sB = [al.f(4 * LC), al.f(4 * LC)]
    sC = [al.f(4 * LC), al.f(4 * LC)]
    sU2 = [al.f(4 * LC)]
    sS2 = [al.f(4 * LC)]
    sDel = [al.f(4 * LC), al.f(4 * LC)]
    sDu = [al.f(4 * LC), al.f(4 * LC)]
    sDt = al.b(4096)
    sR4 = [al.f(4 * LC) for _ in range(4)]
    sYg = al.b(4 * LC)
    sWo = [al.f(LC), al.f(LC)]
    sT0 = al.f(LC)
    sXb = al.f(LC)
    sLn = al.f(LC)
    sG1 = sT0
    sG2 = sXb
    sWoT = al.b(4 * 1024)
    sWdtT = al.b(512)
    sAl = s_small.ap()[:, 162:226]
    assert al.c <= ARN_COLS, al.c

    ld_dt0 = P.dma("gpsimd", lambda e: e.dma_start(
        out=sDt[0:64, :], in_=b_xdf[0:64, 0:4096]), [ph2_gate], "p2.lddta")
    ld_dt1 = P.dma("gpsimd", lambda e: e.dma_start(
        out=sDt[64:128, :], in_=b_xdf[0:64, 4096:8192]), [ph2_gate], "p2.lddtb")
    ld_al = P.dma("sync", lambda e: e.dma_start(out=sAl, in_=alog_t),
                  [ph2_gate], "p2.ldal")
    ld_bdt = P.dma("sync", lambda e: e.dma_start(
        out=s_small.ap()[:, 64:68], in_=bdt_t), [ph2_gate], "p2.ldbdt")
    ld_d = P.dma("sync", lambda e: e.dma_start(
        out=s_small.ap()[:, 68:72], in_=d_t), [ph2_gate], "p2.ldd")
    ld_wo = P.dma("gpsimd", lambda e: e.dma_start(
        out=sWoT.rearrange("p (k m) -> p k m", m=1024),
        in_=w_outT.rearrange("(k p) m -> p k m", p=128)), [ph2_gate], "p2.ldwo")
    ld_wdt0 = P.dma("gpsimd", lambda e: e.dma_start(
        out=sWdtT[0:64, :], in_=w_dtT), [ph2_gate], "p2.ldwdt")
    ld_wdt1 = P.dma("gpsimd", lambda e: e.dma_start(
        out=sWdtT[64:128, :], in_=w_dtT), [ph2_gate], "p2.ldwdtb")
    zms = []
    for par in range(2):
        zms.append(P.op("vector", lambda e, p=par: e.memset(
            sX[p].rearrange("p (n c) -> p n c", c=LCP)[:, :, 0], 0.0),
            [ph2_gate], (), "c", f"p2.zx{par}"))
    zc = P.op("vector", lambda e: e.memset(carry[:, 0:64], 0.0),
              [ph2_gate], (), "c", "p2.zc")
    t_eA = P.op("scalar", lambda e: e.activation(
        out=negA, in_=sAl[:, 0:64], func=AF.Exp), [ld_al], (), "c", "p2.eA")
    t_nA = P.op("vector", lambda e: e.tensor_scalar(
        out=negA, in0=negA, scalar1=-1.0, scalar2=None, op0=ALU.mult),
        [t_eA], (), "c", "p2.nA")

    prev = {}
    wop_stores = []
    stage_idx = 0
    for cc in range(NCH):
        lo = cc * LC
        r, off = cc // 4, (cc % 4) * LC
        ppar = 0
        udeps = [ph2_gate]
        for k in (("du", ppar), ("gate_u", ppar)):
            if k in prev:
                udeps.append(prev[k])
        t_ldu = P.dma("sync", lambda e, o=lo, p=ppar: e.dma_start(
            out=sU2[p].rearrange("p (t c) -> p t c", t=4),
            in_=b_u.ap().rearrange("p (t c) -> p t c", t=4)[:, :, o:o + LC]),
            udeps, f"p2.ldu{cc}", stream=f"p2.ldu{cc % 2}")
        sdeps = [ph2_gate]
        if ("sres_tt", ppar) in prev:
            sdeps.append(prev[("sres_tt", ppar)])
        t_lds = P.dma("sync", lambda e, o=lo, p=ppar: e.dma_start(
            out=sS2[p].rearrange("p (t c) -> p t c", t=4),
            in_=b_sres.ap().rearrange("p (t c) -> p t c", t=4)[:, :, o:o + LC]),
            sdeps, f"p2.lds{cc}", stream=f"p2.lds{cc % 2}")
        for t in range(4):
            pp = [p_a, p_b][t % 2]
            dhalf = 0 if cc < 8 else 64
            rhs_off = (cc % 8) * LC
            mdeps = [ld_dt0, ld_dt1, ld_wdt0, ld_wdt1]
            if ("dpsum", t % 2) in prev:
                mdeps.append(prev[("dpsum", t % 2)])
            mm = P.op("tensor", lambda e, t_=t, p_=pp, dh=dhalf, ro=rhs_off: e.matmul(
                p_.ap()[:, 0:LC],
                sWdtT[dh:dh + 64, t_ * 128:(t_ + 1) * 128],
                sDt[dh:dh + 64, ro:ro + LC],
                start=True, stop=True),
                mdeps, (), "c", f"p2.dmm{cc}.{t}")
            adeps = [mm, ld_bdt]
            if ("chain",) in prev:
                adeps.append(prev[("chain",)])
            if ("deltatile", t, cc % 2) in prev:
                adeps.append(prev[("deltatile", t, cc % 2)])
            if ("gates_done",) in prev:
                adeps.append(prev[("gates_done",)])
            a1 = P.op("scalar", lambda e, t_=t, p_=pp: e.activation(
                out=sT0[:, :], in_=p_.ap()[:, 0:LC], func=AF.Abs,
                bias=s_small.ap()[:, 64 + t_:65 + t_]), adeps, (), "c",
                f"p2.abs{cc}.{t}")
            a3 = P.op("scalar", lambda e: e.activation(
                out=sLn[:, :], in_=sT0[:, :], func=AF.Exp, scale=-1.0),
                [a1], (), "c", f"p2.exp{cc}.{t}")
            a4 = P.op("scalar", lambda e: e.activation(
                out=sLn[:, :], in_=sLn[:, :], func=AF.Ln, bias=1.0),
                [a3], (), "c", f"p2.ln{cc}.{t}")
            v1 = P.op("vector", lambda e, t_=t, p_=pp: e.tensor_scalar(
                out=sXb[:, :], in0=p_.ap()[:, 0:LC],
                scalar1=s_small.ap()[:, 64 + t_:65 + t_], scalar2=0.0,
                op0=ALU.add, op1=ALU.max),
                [a1], (), "c", f"p2.relu{cc}.{t}")
            prev[("dpsum", t % 2)] = v1
            v2 = P.op("vector", lambda e, t_=t, dp=cc % 2: e.tensor_tensor(
                out=sDel[dp][:, t_ * LC:(t_ + 1) * LC], in0=sXb[:, :],
                in1=sLn[:, :], op=ALU.add),
                [v1, a4], (), "c", f"p2.del{cc}.{t}")
            prev[("chain",)] = v2
            prev[("delta", t)] = v2
        dudeps = [prev[("delta", t)] for t in range(4)] + [t_ldu]
        if ("dbu_last",) in prev:
            dudeps.append(prev[("dbu_last",)])
        t_du = P.op("vector", lambda e, p=ppar, dp=cc % 2: e.tensor_tensor(
            out=sDu[dp][:, :], in0=sDel[dp][:, :], in1=sU2[p][:, :], op=ALU.mult),
            dudeps, (), "c", f"p2.du{cc}")
        prev[("du", ppar)] = t_du
        for q in range(4):
            qpar = q % 2
            bdeps = [cc_xd, ph2_gate]
            if ("dbu_q", qpar) in prev:
                bdeps.append(prev[("dbu_q", qpar)])
            t_ldb = P.dma("sync", lambda e, q_=q, o=lo, p=qpar: e.dma_start(
                out=sB[p].rearrange("p (n c) -> p n c", n=4),
                in_=b_xdf.ap()[64 + q_ * 4:68 + q_ * 4, o:o + LC]
                    .unsqueeze(0).partition_broadcast(128)),
                bdeps, f"p2.ldb{cc}.{q}", stream=f"p2.ldb{qpar}")
            cdeps = [cc_xd, ph2_gate]
            if ("hc_q", qpar) in prev:
                cdeps.append(prev[("hc_q", qpar)])
            t_ldc = P.dma("sync", lambda e, q_=q, o=lo, p=qpar: e.dma_start(
                out=sC[p].rearrange("p (n c) -> p n c", n=4),
                in_=b_xdf.ap()[80 + q_ * 4:84 + q_ * 4, o:o + LC]
                    .unsqueeze(0).partition_broadcast(128)),
                cdeps, f"p2.ldc{cc}.{q}", stream=f"p2.ldc{qpar}")
            for t in range(4):
                sp = stage_idx % 2
                adeps = [prev[("delta", t)], t_nA, zms[sp]]
                if ("scan_x", sp) in prev:
                    adeps.append(prev[("scan_x", sp)])
                da = None
                for n in range(4):
                    da = P.op("scalar", lambda e, t_=t, q_=q, n_=n, s=sp, dp=cc % 2: e.activation(
                        out=sX[s][:, n_ * LCP + 1:(n_ + 1) * LCP],
                        in_=sDel[dp][:, t_ * LC:(t_ + 1) * LC], func=AF.Exp,
                        scale=negA[:, t_ * 16 + q_ * 4 + n_:t_ * 16 + q_ * 4 + n_ + 1]),
                        adeps if n == 0 else [], (), "c", f"p2.dA{cc}.{q}.{t}.{n}")
                if q == 3:
                    prev[("deltatile", t, cc % 2)] = da
                budeps = [t_du, t_ldb]
                if ("y_hc", sp) in prev:
                    budeps.append(prev[("y_hc", sp)])
                if ("y_cp", sp) in prev:
                    budeps.append(prev[("y_cp", sp)])
                dbu = P.op("gpsimd", lambda e, t_=t, s=sp, p=qpar, dp=cc % 2: e.tensor_tensor(
                    out=sY[s].rearrange("p (n c) -> p n c", c=LCP)[:, :, 1:LCP],
                    in0=sDu[dp][:, t_ * LC:(t_ + 1) * LC].unsqueeze(1)
                        .broadcast_to([128, 4, LC]),
                    in1=sB[p].rearrange("p (n c) -> p n c", n=4), op=ALU.mult),
                    budeps, (), "c", f"p2.dbu{cc}.{q}.{t}")
                prev[("dbu_q", qpar)] = dbu
                prev[("dbu_last",)] = dbu
                cideps = [zc]
                if ("cpx", t, q) in prev:
                    cideps.append(prev[("cpx", t, q)])
                if ("y_hc", sp) in prev:
                    cideps.append(prev[("y_hc", sp)])
                if ("y_cp", sp) in prev:
                    cideps.append(prev[("y_cp", sp)])
                ci = P.op("gpsimd", lambda e, t_=t, q_=q, s=sp: e.tensor_copy(
                    sY[s].rearrange("p (n c) -> p n c", c=LCP)[:, :, 0],
                    carry[:, t_ * 16 + q_ * 4:t_ * 16 + q_ * 4 + 4]),
                    cideps, (), "c", f"p2.ci{cc}.{q}.{t}")
                scdeps = [da, dbu, ci]
                sc = P.op("vector", lambda e, s=sp: e.tensor_tensor_scan(
                    out=sY[s][:, :], data0=sX[s][:, :], data1=sY[s][:, :],
                    initial=0.0, op0=ALU.mult, op1=ALU.add),
                    scdeps, (), "c", f"p2.sc{cc}.{q}.{t}")
                prev[("scan_x", sp)] = sc
                hcdeps = [sc, t_ldc]
                if ("rd_hc", sp) in prev:
                    hcdeps.append(prev[("rd_hc", sp)])
                hc = P.op("gpsimd" if t != 0 else "vector",
                    lambda e, s=sp, p=qpar: e.tensor_tensor(
                    out=sHC[s][:, :],
                    in0=sY[s].rearrange("p (n c) -> p n c", c=LCP)[:, :, 1:LCP],
                    in1=sC[p].rearrange("p (n c) -> p n c", n=4), op=ALU.mult),
                    hcdeps, (), "c", f"p2.hc{cc}.{q}.{t}")
                cp = P.op("gpsimd", lambda e, t_=t, q_=q, s=sp: e.tensor_copy(
                    carry[:, t_ * 16 + q_ * 4:t_ * 16 + q_ * 4 + 4],
                    sY[s].rearrange("p (n c) -> p n c", c=LCP)[:, :, LCP - 1]),
                    [sc], (), "c", f"p2.cp{cc}.{q}.{t}")
                prev[("cpx", t, q)] = cp
                prev[("y_hc", sp)] = hc
                prev[("y_cp", sp)] = cp
                prev[("hc_q", qpar)] = hc
                rddeps = [hc]
                if ("rd_war", q, t) in prev:
                    rddeps.append(prev[("rd_war", q, t)])
                rd_spec = (rddeps, sp, q, t, cc)
                if ("rd_pending",) in prev:
                    prev[("rd_pending",)]()
                def mk_rd(spec=rd_spec):
                    deps, s_, q_, t_, c_ = spec
                    rd = P.op("vector", lambda e, s=s_, qq=q_, tq=t_: e.tensor_reduce(
                        out=sR4[qq][:, tq * LC:(tq + 1) * LC],
                        in_=sHC[s].rearrange("p (n c) -> p c n", n=4),
                        axis=AX.X, op=ALU.add),
                        deps, (), "c", f"p2.rd{c_}.{q_}.{t_}")
                    prev[("rd_hc", s_)] = rd
                    prev[("rd", t_, q_)] = rd
                    prev.pop(("rd_pending",), None)
                prev[("rd_pending",)] = mk_rd
                stage_idx += 1
        if ("rd_pending",) in prev:
            prev[("rd_pending",)]()
        for t in range(4):
            gdeps = [prev[("rd", t, q)] for q in range(4)] + [t_ldu, ld_d]
            if ("wout_mm",) in prev:
                gdeps.append(prev[("wout_mm",)])
            g1 = P.op("vector", lambda e, t_=t, p=ppar: e.scalar_tensor_tensor(
                out=sG1[:, :], in0=sU2[p][:, t_ * LC:(t_ + 1) * LC],
                scalar=s_small.ap()[:, 68 + t_:69 + t_],
                in1=sR4[0][:, t_ * LC:(t_ + 1) * LC],
                op0=ALU.mult, op1=ALU.add), gdeps, (), "c", f"p2.g1{cc}.{t}")
            g2 = P.op("vector", lambda e, t_=t: e.tensor_tensor(
                out=sG2[:, :], in0=sG1[:, :],
                in1=sR4[1][:, t_ * LC:(t_ + 1) * LC], op=ALU.add),
                [g1], (), "c", f"p2.g2{cc}.{t}")
            g3 = P.op("vector", lambda e, t_=t: e.tensor_tensor(
                out=sG1[:, :], in0=sG2[:, :],
                in1=sR4[2][:, t_ * LC:(t_ + 1) * LC], op=ALU.add),
                [g2], (), "c", f"p2.g3{cc}.{t}")
            g4 = P.op("vector", lambda e, t_=t: e.tensor_tensor(
                out=sG2[:, :], in0=sG1[:, :],
                in1=sR4[3][:, t_ * LC:(t_ + 1) * LC], op=ALU.add),
                [g3], (), "c", f"p2.g4{cc}.{t}")
            yg = P.op("vector", lambda e, t_=t, p=ppar: e.tensor_tensor(
                out=sYg[:, t_ * LC:(t_ + 1) * LC], in0=sG2[:, :],
                in1=sS2[p][:, t_ * LC:(t_ + 1) * LC], op=ALU.mult),
                [g4, t_lds], (), "c", f"p2.yg{cc}.{t}")
            prev[("gate_u", ppar)] = g1
            prev[("sres_tt", ppar)] = yg
            prev[("gates_done",)] = yg
            prev[("yg", t)] = yg
            for q in range(4):
                prev[("rd_war", q, t)] = g4 if q >= 2 else g2
        for m in range(8):
            pp = [p_c, p_d][m % 2]
            wdeps = [prev[("yg", t)] for t in range(4)] + [ld_wo]
            if ("wo_ev", m % 2) in prev:
                wdeps.append(prev[("wo_ev", m % 2)])
            mm = None
            for k in range(4):
                mm = P.op("tensor", lambda e, k_=k, m_=m, p_=pp: e.matmul(
                    p_.ap()[:, 0:LC],
                    sWoT[:, k_ * 1024 + m_ * 128:k_ * 1024 + (m_ + 1) * 128],
                    sYg[:, k_ * LC:(k_ + 1) * LC],
                    start=(k_ == 0), stop=(k_ == 3)),
                    wdeps if k == 0 else [], (), "c", f"p2.wmm{cc}.{m}.{k}")
            prev[("wout_mm",)] = mm
            evdeps = [mm]
            if ("wo_st", m % 2) in prev:
                evdeps.append(prev[("wo_st", m % 2)])
            ev = P.op("scalar", lambda e, m_=m, p_=pp: e.activation(
                out=sWo[m_ % 2][:, :], in_=p_.ap()[:, 0:LC], func=AF.Copy),
                evdeps, (), "c", f"p2.wev{cc}.{m}")
            prev[("wo_ev", m % 2)] = ev
            st_ = P.dma("sync", lambda e, m_=m, r_=r, j_=cc % 4: e.dma_start(
                out=b_wopj[j_].ap()[r_ * DM + m_ * 128:r_ * DM + (m_ + 1) * 128, :],
                in_=sWo[m_ % 2][:, :]),
                [ev], f"p2.wst{cc}.{m}", stream=f"p2.wst{m % 2}")
            prev[("wo_st", m % 2)] = st_
            wop_stores.append(st_)

    cc_rsj = []
    for j in range(4):
        jdeps = [wop_stores[8 * c + m] for c in range(j, NCH, 4) for m in range(8)]
        cc_rsj.append(P.cc(lambda e, j_=j: e.collective_compute(
            "ReduceScatter", ALU.add, replica_groups=GROUPS,
            ins=[b_wopj[j_].ap().opt()], outs=[b_rsj[j_].ap().opt()]),
            jdeps, f"cc.rs{j}"))
    cc_rs = cc_rsj[-1]

    if upto < 3:
        P.emit(nc, bank, final=list(taps.values()) + cc_rsj +
               ([] if skip_wq else [cc_wg, cc_wu, cc_wd]))
        return nc, taps

    P.emit(nc, bank)
    nc.all_engine_barrier()
    P3 = Prog()
    P = P3

    al = Al()
    sXqfm = al.b(16 * 1024)
    scr2 = dict(mx=al.f(3 * (IFF // 16)), q1=al.f(IFF), q2=al.f(IFF),
                q3=al.f(IFF))
    ph4_base = al.c
    sAt = [al.f(1024), al.f(1024)]
    sX3 = [al.f(1024), al.f(1024)]
    sRes = [al.f(1024), al.f(1024)]
    sSq3 = [al.f(1024), al.f(1024)]
    sXn2 = [al.f(1024), al.f(1024)]
    sXq = [al.b(1024), al.b(1024)]
    assert al.c <= ARN_COLS, al.c
    scr3 = [dict(mx=scr2["mx"][:, 0:192], q1=scr2["q1"][:, 0:1024],
                 q2=scr2["q2"][:, 0:1024], q3=scr2["q3"][:, 0:1024]),
            dict(mx=scr2["mx"][:, 192:384], q1=scr2["q1"][:, 1024:2048],
                 q2=scr2["q2"][:, 1024:2048], q3=scr2["q3"][:, 1024:2048])]
    stats3 = [s_small.ap()[:, 140:148], s_small.ap()[:, 148:156]]

    ph3_gate = P.op("sync", lambda e: e.nop(), [wop_stores[-1]], (),
                    "c", "ph3.gate")
    prev = {}
    res_stores = []

    def ph3_tile(tt):
        par = tt % 2
        adeps = [ph3_gate, cc_rsj[tt // 4]]
        if ("tr_last", par) in prev:
            adeps.append(prev[("tr_last", par)])
        t_lda = P.dma("sync", lambda e, t=tt, p=par: e.dma_start(
            out=sAt[p].rearrange("p (k c) -> p k c", c=128),
            in_=b_rsj[t // 4].ap()[:, (t % 4) * 128:(t % 4 + 1) * 128]
                .rearrange("(k p) c -> p k c", p=128)),
            adeps, f"p3.lda{tt}", stream=f"p3.lda{par}")
        xdeps = [ph3_gate]
        if ("resid", par) in prev:
            xdeps.append(prev[("resid", par)])
        t_ldx = P.dma("sync", lambda e, t=tt, p=par: e.dma_start(
            out=sX3[p], in_=x_t[t * 128:(t + 1) * 128, :]),
            xdeps, f"p3.ldx{tt}", stream=f"p3.ldx{par}")
        yield
        revs = []
        for kt in range(8):
            p2_ = kt % 2
            tdeps = [t_lda, ld_idf]
            if ("pt", p2_) in xblock:
                xblock[("pt", p2_)].sig = True
                tdeps.append(xblock[("pt", p2_)])
            t_tr = P.op("tensor", lambda e, k=kt, p=par, p_=p2_: e.transpose(
                pt[p_], sAt[p][:, k * 128:(k + 1) * 128], s_idf.ap()),
                tdeps, (), "c", f"p3.tr{tt}.{kt}")
            evdeps = [t_tr, t_ldx]
            if ("res_store", par) in prev and kt == 0:
                evdeps.append(prev[("res_store", par)])
            if ("ln_done", par) in prev and kt == 0:
                evdeps.append(prev[("ln_done", par)])
            t_ev = P.op("vector", lambda e, k=kt, p=par, p_=p2_: e.tensor_tensor(
                out=sRes[p][:, k * 128:(k + 1) * 128],
                in0=sX3[p][:, k * 128:(k + 1) * 128], in1=pt[p_], op=ALU.add),
                evdeps, (), "c", f"p3.ev{tt}.{kt}")
            xblock[("pt", p2_)] = t_ev
            revs.append(t_ev)
            if kt % 2 == 1:
                yield
        prev[("tr_last", par)] = revs[-1]
        t_res = revs[-1]
        prev[("resid", par)] = t_res
        t_rst = P.dma("sync", lambda e, t=tt, p=par: e.dma_start(
            out=b_res[t * 128:(t + 1) * 128, :], in_=sRes[p]),
            revs, f"p3.rst{tt}", stream=f"p3.rst{par}")
        res_stores.append(t_rst)
        prev[("res_store", par)] = t_rst
        yield
        f_ops, t_xn2 = ln_ops(P, sRes[par], sXn2[par], stats3[par], sSq3[par],
                              f"p3.ln{tt}", consts)
        for f in f_ops:
            f.deps += revs
            if ("ln_done", par) in prev:
                f.deps.append(prev[("ln_done", par)])
        yield
        q0, qo = quant_ops(P, None, sXn2[par], sXq[par], scr3[par])
        q0.deps.append(t_xn2)
        if ("xq_tr", par) in prev:
            qo.deps.append(prev[("xq_tr", par)])
        prev[("ln_done", par)] = t_xn2
        yield
        for kt in range(8):
            p2_ = kt % 2
            tdeps = [qo, ld_id]
            if ("pt", p2_) in xblock:
                tdeps.append(xblock[("pt", p2_)])
            t_tr = P.op("tensor", lambda e, k=kt, p=par, p_=p2_: e.transpose(
                ptb[p_], sXq[p][:, k * 128:(k + 1) * 128], s_id.ap()),
                tdeps, (), "c", f"p3.qtr{tt}.{kt}")
            t_ev = P.op("vector", lambda e, k=kt, t_=tt, p_=p2_: e.tensor_copy(
                sXqfm[:, (k * 16 + t_) * 128:(k * 16 + t_) * 128 + 128],
                ptb[p_]), [t_tr], (), "c", f"p3.qev{tt}.{kt}")
            xblock[("pt", p2_)] = t_ev
            if kt == 7:
                prev[("xq_tr", par)] = t_ev
            if kt % 2 == 1:
                yield
        prev[("xqdone", tt)] = prev[("xq_tr", par)]

    for base in range(0, 16, 2):
        g0, g1 = ph3_tile(base), ph3_tile(base + 1)
        done0 = done1 = False
        while not (done0 and done1):
            if not done0:
                try:
                    next(g0)
                except StopIteration:
                    done0 = True
            if not done1:
                try:
                    next(g1)
                except StopIteration:
                    done1 = True

    for t in range(16):
        prev[("xqdone", t)].sig = True
    P.emit(nc, bank)
    nc.all_engine_barrier()
    P4 = Prog()
    P = P4

    al = Al(ph4_base)
    sWqg = al.b(22528)
    sWqu = al.b(22528)
    sH = al.f(IFF)
    sSg = al.f(1024)
    sHq = [al.b(IFF), al.b(IFF)]
    sHqT = [al.b(22 * 128), al.b(22 * 128)]
    assert al.c <= ARN_COLS, al.c

    g4deps = [prev[("xqdone", t)] for t in range(16)] + res_stores[-2:]
    if not skip_wq:
        g4deps += [cc_wg, cc_wu]
    ph4_gate = P.op("sync", lambda e: e.nop(), g4deps, (), "c", "ph4.gate")
    ld_wg = ld_wu = None
    for rr in range(8):
        ld_wg = P.dma("sync", lambda e, r_=rr: e.dma_start(
            out=sWqg.rearrange("p (k s) -> p k s", k=8)[:, :, r_ * IFS:(r_ + 1) * IFS],
            in_=b_wqg_g.ap()[r_ * DM:(r_ + 1) * DM, :]
                .rearrange("(k p) s -> p k s", p=128)),
            [ph4_gate], f"p4.ldwg{rr}", stream="p4.ldwg")
        ld_wu = P.dma("sync", lambda e, r_=rr: e.dma_start(
            out=sWqu.rearrange("p (k s) -> p k s", k=8)[:, :, r_ * IFS:(r_ + 1) * IFS],
            in_=b_wqu_g.ap()[r_ * DM:(r_ + 1) * DM, :]
                .rearrange("(k p) s -> p k s", p=128)),
            [ph4_gate], f"p4.ldwu{rr}", stream="p4.ldwu")

    NCC = [(0, 512), (512, 512), (1024, 512), (1536, 512), (2048, 512), (2560, 256)]
    hq_stores = []
    prev4 = {}
    qo_t = {}

    def ph4a_tail(tp):
        par = tp % 2
        evs = []
        for kt in range(22):
            p2_ = kt % 2
            tdeps = [qo_t[tp], ld_id]
            if ("pt", p2_) in xblock:
                xblock[("pt", p2_)].sig = True
                tdeps.append(xblock[("pt", p2_)])
            t_tr = P.op("tensor", lambda e, k=kt, p=par, p_=p2_: e.transpose(
                ptb[p_], sHq[p][:, k * 128:(k + 1) * 128], s_id.ap()),
                tdeps, (), "c", f"p4.htr{tp}.{kt}")
            evdeps = [t_tr]
            if ("hq_store", par) in prev4 and kt == 0:
                evdeps.append(prev4[("hq_store", par)])
            t_ev = P.op("scalar", lambda e, k=kt, p=par, p_=p2_: e.copy(
                sHqT[p][:, k * 128:(k + 1) * 128], ptb[p_]),
                evdeps, (), "c", f"p4.hev{tp}.{kt}")
            xblock[("pt", p2_)] = t_ev
            evs.append(t_ev)
        prev4[("hqtr_last", par)] = evs[-1]
        t_hst = P.dma("sync", lambda e, t=tp, p=par: e.dma_start(
            out=b_hq.ap()[:, t * 128:(t + 1) * 128]
                .rearrange("(k p) c -> p k c", p=128),
            in_=sHqT[p].rearrange("p (k c) -> p k c", c=128)),
            evs, f"p4.hst{tp}", stream="p4.hst")
        hq_stores.append(t_hst)
        prev4[("hq_store", par)] = t_hst

    for tt in range(16):
        for ci4, (j0, ncw) in enumerate(NCC):
            pga, pgb = ([p_a, p_b], [p_c, p_d])[ci4 % 2]
            gdeps = [ld_wg, ph4_gate]
            if ("silu", ci4 % 2) in prev4:
                gdeps.append(prev4[("silu", ci4 % 2)])
            mmg = None
            for kt in range(8):
                mmg = P.op("tensor", lambda e, k=kt, t_=tt, j=j0, w=ncw, p_=pga: e.matmul(
                    p_.ap()[:, 0:w],
                    sXqfm[:, (k * 16 + t_) * 128:(k * 16 + t_) * 128 + 128],
                    sWqg[:, k * 2816 + j:k * 2816 + j + w],
                    start=(k == 0), stop=(k == 7)),
                    gdeps if kt == 0 else [], (), "c", f"p4.gmm{tt}.{j0}.{kt}")
            udeps = [ld_wu]
            if ("hmul", ci4 % 2) in prev4:
                udeps.append(prev4[("hmul", ci4 % 2)])
            mmu = None
            for kt in range(8):
                mmu = P.op("tensor", lambda e, k=kt, t_=tt, j=j0, w=ncw, p_=pgb: e.matmul(
                    p_.ap()[:, 0:w],
                    sXqfm[:, (k * 16 + t_) * 128:(k * 16 + t_) * 128 + 128],
                    sWqu[:, k * 2816 + j:k * 2816 + j + w],
                    start=(k == 0), stop=(k == 7)),
                    udeps if kt == 0 else [], (), "c", f"p4.umm{tt}.{j0}.{kt}")
            sdeps = [mmg]
            if ("hmul", ci4 % 2) in prev4:
                sdeps.append(prev4[("hmul", ci4 % 2)])
            t_sg = P.op("scalar", lambda e, w=ncw, p_=pga, sgo=(ci4 % 2) * 512: e.activation(
                out=sSg[:, sgo:sgo + w], in_=p_.ap()[:, 0:w], func=AF.Silu),
                sdeps, (), "c", f"p4.sg{tt}.{j0}")
            prev4[("silu", ci4 % 2)] = t_sg
            t_h = P.op("vector", lambda e, j=j0, w=ncw, p_=pgb, sgo=(ci4 % 2) * 512: e.tensor_tensor(
                out=sH[:, j:j + w], in0=sSg[:, sgo:sgo + w], in1=p_.ap()[:, 0:w],
                op=ALU.mult), [t_sg, mmu], (), "c", f"p4.h{tt}.{j0}")
            prev4[("hmul", ci4 % 2)] = t_h
            prev4[("h", j0)] = t_h
        q0, qo = quant_ops(P, None, sH, sHq[tt % 2], scr2)
        q0.deps += [prev4[("h", j0)] for (j0, _) in NCC]
        if ("hqtr_last", tt % 2) in prev4:
            qo.deps.append(prev4[("hqtr_last", tt % 2)])
        qo_t[tt] = qo
        if tt >= 1:
            ph4a_tail(tt - 1)
    ph4a_tail(15)

    P.emit(nc, bank)
    nc.all_engine_barrier()
    P5 = Prog()
    P = P5

    al = Al(ph4_base)
    sWqd = al.b(22528)
    sHqL = [al.b(2816), al.b(2816)]
    sRes2 = [al.f(1024), al.f(1024)]
    sOut = [al.f(1024), al.f(1024)]
    assert al.c <= ARN_COLS, al.c

    g5deps = [hq_stores[-1]]
    if not skip_wq:
        g5deps += [cc_wd]
    ph5_gate = P.op("sync", lambda e: e.nop(), g5deps, (), "c", "ph5.gate")
    ld_wd = None
    for rr in range(8):
        ld_wd = P.dma("sync", lambda e, r_=rr: e.dma_start(
            out=sWqd.rearrange("p (k c) -> p k c", k=22)[:, :, r_ * 128:(r_ + 1) * 128],
            in_=b_wqd_g.ap()[r_ * IFF:(r_ + 1) * IFF, :]
                .rearrange("(k p) c -> p k c", p=128)),
            [ph5_gate], f"p5.ldwd{rr}", stream="p5.ldwd")
    prev5 = {}
    out_stores = []
    for tt in range(16):
        par = tt % 2
        hdeps = [ph5_gate]
        if ("dmm_last", par) in prev5:
            hdeps.append(prev5[("dmm_last", par)])
        t_ldh = P.dma("sync", lambda e, t=tt, p=par: e.dma_start(
            out=sHqL[p].rearrange("p (k c) -> p k c", c=128),
            in_=b_hq.ap()[:, t * 128:(t + 1) * 128]
                .rearrange("(k p) c -> p k c", p=128)),
            hdeps, f"p5.ldh{tt}", stream=f"p5.ldh{tt % 2}")
        rdeps = [ph5_gate]
        if ("fin", par) in prev5:
            rdeps.append(prev5[("fin", par)])
        t_ldr2 = P.dma("sync", lambda e, t=tt, p=par: e.dma_start(
            out=sRes2[p], in_=b_res[t * 128:(t + 1) * 128, :]),
            rdeps, f"p5.ldr{tt}", stream=f"p5.ldr{tt % 2}")
        mdeps = [t_ldh, ld_wd]
        pbanks = [[p_e.ap()[:, 0:512], p_e.ap()[:, 512:1024]],
                  [p_a.ap()[:, 0:512], p_b.ap()[:, 0:512]]][par]
        mm = None
        mms = []
        for hh in range(2):
            if ("fin_h", par, hh) in prev5:
                mdeps_h = (mdeps + [prev5[("fin_h", par, hh)]]) if True else mdeps
            else:
                mdeps_h = mdeps
            for kt in range(22):
                mm = P.op("tensor", lambda e, k=kt, p=par, h=hh, pb=pbanks: e.matmul(
                    pb[h],
                    sHqL[p][:, k * 128:(k + 1) * 128],
                    sWqd[:, k * 1024 + h * 512:k * 1024 + (h + 1) * 512],
                    start=(k == 0), stop=(k == 21)),
                    mdeps_h if kt == 0 else [], (), "c",
                    f"p5.dmm{tt}.{hh}.{kt}")
            mms.append(mm)
        prev5[("dmm_last", par)] = mm
        t_fin = None
        for hh in range(2):
            fdeps = [mms[hh], t_ldr2]
            if ("out_st", par) in prev5 and hh == 0:
                fdeps.append(prev5[("out_st", par)])
            t_fin = P.op("vector", lambda e, p=par, h=hh, pb=pbanks: e.tensor_tensor(
                out=sOut[p][:, h * 512:(h + 1) * 512], in0=pb[h],
                in1=sRes2[p][:, h * 512:(h + 1) * 512], op=ALU.add),
                fdeps, (), "c", f"p5.fin{tt}.{hh}")
            prev5[("fin_h", par, hh)] = t_fin
        prev5[("fin", par)] = t_fin
        t_ost = P.dma("sync", lambda e, t=tt, p=par: e.dma_start(
            out=o_y[t * 128:(t + 1) * 128, :], in_=sOut[p]),
            [t_fin], f"p5.ost{tt}", stream=f"p5.ost{tt % 2}")
        prev5[("out_st", par)] = t_ost
        out_stores.append(t_ost)

    P.emit(nc, bank, final=list(taps.values()) + out_stores)
    return nc, taps
    return nc, taps


def prep_inputs(inputs, c):
    import ml_dtypes
    f32 = np.float32
    beta, j = c // 4, c % 4
    x = np.asarray(inputs["L0_attn_x"], f32)
    W_in = np.asarray(inputs["L0_attn_W_in"], f32)
    W_conv = np.asarray(inputs["L0_attn_W_conv"], f32)
    b_conv = np.asarray(inputs["L0_attn_b_conv"], f32)
    W_xproj = np.asarray(inputs["L0_attn_W_xproj"], f32)
    W_dt = np.asarray(inputs["L0_attn_W_dt"], f32)
    b_dt = np.asarray(inputs["L0_attn_b_dt"], f32)
    A_log = np.asarray(inputs["L0_attn_A_log"], f32)
    D = np.asarray(inputs["L0_attn_D"], f32)
    W_out = np.asarray(inputs["L0_attn_W_out"], f32)
    wg = np.asarray(inputs["L0_ffn_w_gate"], f32)
    wu = np.asarray(inputs["L0_ffn_w_up"], f32)
    wd = np.asarray(inputs["L0_ffn_w_down"], f32)

    sl = slice(j * DS, (j + 1) * DS)
    dconv = np.zeros((128, 4, 4, 128), f32)
    for m in range(4):
        for k in range(4):
            dconv[np.arange(128), m, k, np.arange(128)] = \
                W_conv[j * DS + m * 128:j * DS + (m + 1) * 128, 0, k]
    d = dict(
        x_t=np.ascontiguousarray(x[beta, (c % 4) * TOK:(c % 4 + 1) * TOK, :]),
        w_inT=np.ascontiguousarray(
            np.concatenate([W_in[sl], W_in[DI + j * DS:DI + (j + 1) * DS]], 0).T),
        dconv=np.ascontiguousarray(dconv.reshape(128, 16 * 128)),
        bconv_t=np.ascontiguousarray(b_conv[sl].reshape(4, 128).T),
        w_xpT=np.ascontiguousarray(W_xproj[:, sl].T),
        w_dtT=np.ascontiguousarray(W_dt[sl, :].T),
        bdt_t=np.ascontiguousarray(b_dt[sl].reshape(4, 128).T),
        alog_t=np.ascontiguousarray(
            A_log[sl].reshape(4, 128, NST).transpose(1, 0, 2).reshape(128, 4 * NST)),
        d_t=np.ascontiguousarray(D[sl].reshape(4, 128).T),
        w_outT=np.ascontiguousarray(W_out[:, sl].T),
        wg_sh=np.ascontiguousarray(wg[c * IFS:(c + 1) * IFS, :]),
        wu_sh=np.ascontiguousarray(wu[c * IFS:(c + 1) * IFS, :]),
        wd_sh=np.ascontiguousarray(wd[c * WDS:(c + 1) * WDS, :]),
        ident=np.eye(128, dtype=ml_dtypes.bfloat16),
        identf=np.eye(128, dtype=np.float32),
    )
    return d


_NC_CACHE = {}


def kernel(**inputs):
    key = "k"
    if key not in _NC_CACHE:
        _NC_CACHE[key] = build(upto=4)
    nc, _ = _NC_CACHE[key]
    in_maps = [prep_inputs(inputs, c) for c in range(8)]
    res = run_bass_kernel_spmd(nc, in_maps, core_ids=list(range(8)))
    out = np.empty((B, L, DM), np.float32)
    for c in range(8):
        beta, j = c // 4, c % 4
        out[beta, j * TOK:(j + 1) * TOK, :] = res.results[c]["o_y"]
    return out


# version 12 (final)
# speedup vs baseline: 1.0413x; 1.0413x over previous
import numpy as np
import concourse.bass as bass
import concourse.mybir as mybir
from concourse.bass_utils import run_bass_kernel_spmd

F32 = mybir.dt.float32
BF16 = mybir.dt.bfloat16
I32 = mybir.dt.int32
AF = mybir.ActivationFunctionType
ALU = mybir.AluOpType
AX = mybir.AxisListType

B, L, DM, DI, NST, DTR, NE, IFF = 2, 8192, 1024, 2048, 16, 64, 96, 2816
DS = 512
TOK = 2048
LB = 8192
L1C = 512
N1C = LB // L1C
LC = 512
NCH = LB // LC
NQ = 4
NGRP = NST // NQ
GROUPS = [[0, 1, 2, 3], [4, 5, 6, 7]]
WGROUP = [list(range(8))]
IFS = IFF // 8
WDS = DM // 8
MAGIC1 = 12582912.0


ENGINES = ("sync", "vector", "scalar", "tensor", "gpsimd")


class Op:
    __slots__ = ("eng", "fn", "deps", "soft", "kind", "idx", "sig", "val", "name", "sem", "stream")

    def __init__(self, eng, fn, deps, soft, kind, name):
        self.eng = eng
        self.fn = fn
        self.deps = deps
        self.soft = soft
        self.kind = kind
        self.idx = -1
        self.sig = False
        self.val = 0
        self.name = name


class Prog:
    def __init__(self):
        self.ops = {e: [] for e in ENGINES}
        self.all = []

    def op(self, eng, fn, deps=(), soft=(), kind="c", name="", stream=None):
        o = Op(eng, fn, list(deps), set(id(s) for s in soft), kind, name)
        o.idx = len(self.ops[eng])
        o.stream = stream
        self.ops[eng].append(o)
        self.all.append(o)
        return o

    def dma(self, eng, fn, deps=(), name="", stream=None):
        if stream is None:
            stream = "".join(ch for ch in name if not ch.isdigit())
        return self.op(eng, fn, deps, (), "d", name, stream=stream)

    def cc(self, fn, deps=(), name=""):
        return self.op("gpsimd", fn, deps, (), "cc", name)

    def emit(self, nc, bank, final=()):
        if final:
            self.op("sync", lambda e: e.nop(), list(final), (), "c", "terminal")
        for o in self.all:
            for d in o.deps:
                if d.kind == "c" and d.eng == o.eng and id(d) in o.soft:
                    continue
                if hasattr(d, "sem"):
                    if d.kind == "c" and not d.sig:
                        raise RuntimeError(
                            f"dep {d.name} of {o.name} crosses blocks unsignaled")
                    continue
                d.sig = True
        sems, cnt = bank.sems, bank.cnt
        for e in ENGINES:
            for o in self.ops[e]:
                if o.kind == "cc":
                    o.sem = nc.alloc_semaphore(f"scc{bank.ncc}")
                    bank.ncc += 1
                    o.sig = True
                    o.val = 1
                elif o.kind == "d":
                    o.sig = True
                    st = bank.stream(o.stream)
                    if st[2] is not None:
                        o.deps.append(st[2])
                    st[1] += 16
                    o.sem = st[0]
                    o.val = st[1]
                    st[2] = o
                else:
                    if o.sig:
                        cnt[("c", e)] += 1
                    o.sem = sems[("c", e)]
                    o.val = cnt[("c", e)]

        def emit_stream(eng_obj, e):
            waited = {}
            for o in self.ops[e]:
                for d in o.deps:
                    if d.kind == "c" and d.eng == e and id(d) in o.soft:
                        continue
                    if d.kind == "c" and d.eng == e and not d.sig:
                        raise RuntimeError(f"unsignaled dep {d.name} of {o.name}")
                    key = id(d.sem)
                    if waited.get(key, -1) >= d.val:
                        continue
                    eng_obj.wait_ge(d.sem, d.val)
                    waited[key] = d.val
                ins = o.fn(eng_obj)
                if o.kind == "cc":
                    ins.then_inc(o.sem, 1)
                elif o.kind == "d":
                    ins.then_inc(o.sem, 16)
                elif o.sig:
                    if ins is None:
                        raise RuntimeError(f"op {o.name} returned no instruction")
                    ins.then_inc(o.sem, 1)
            return None

        with nc.Block() as blk:
            blk.sync(lambda e: emit_stream(e, "sync"))
            blk.vector(lambda e: emit_stream(e, "vector"))
            blk.scalar(lambda e: emit_stream(e, "scalar"))
            blk.tensor(lambda e: emit_stream(e, "tensor"))
            blk.gpsimd(lambda e: emit_stream(e, "gpsimd"))


class SemBank:
    def __init__(self, nc):
        self.nc = nc
        self.sems = {}
        for e in ENGINES:
            self.sems[("c", e)] = nc.alloc_semaphore(f"sc_{e}")
        self.cnt = {k: 0 for k in self.sems}
        self.streams = {}
        self.ncc = 0

    def stream(self, name):
        if name not in self.streams:
            self.streams[name] = [self.nc.alloc_semaphore(f"sm_{len(self.streams)}"), 0, None]
        return self.streams[name]


def quant_ops(P, v, x_f32, out_bf16, scr):
    p, Fv = x_f32.shape
    nb = Fv // 16
    mx, q1, q2, q3 = scr["mx"], scr["q1"], scr["q2"], scr["q3"]
    o = []

    def vop(fn, deps, name, soft=()):
        o.append(P.op("vector", fn, deps, soft, "c", name))
        return o[-1]

    t_mx = vop(lambda e: e.tensor_reduce(
        out=mx[:p, 0:nb], in_=x_f32.rearrange("p (a b) -> p a b", b=16),
        axis=AX.X, op=ALU.max, apply_absolute_value=True), [], "q.mx")
    t_mxc = vop(lambda e: e.tensor_scalar(
        out=mx[:p, 0:nb], in0=mx[:p, 0:nb], scalar1=6e-12, scalar2=None,
        op0=ALU.max), [t_mx], "q.mxc")
    t_r = vop(lambda e: e.reciprocal(mx[:p, nb:2 * nb], mx[:p, 0:nb]),
              [t_mxc], "q.r")
    t_r6 = vop(lambda e: e.tensor_scalar(
        out=mx[:p, nb:2 * nb], in0=mx[:p, nb:2 * nb], scalar1=6.0,
        scalar2=None, op0=ALU.mult), [t_r], "q.r6")
    t_s6 = vop(lambda e: e.tensor_scalar(
        out=mx[:p, 2 * nb:3 * nb], in0=mx[:p, 0:nb], scalar1=1.0 / 6.0,
        scalar2=None, op0=ALU.mult), [t_mxc], "q.s6")
    t_xn = vop(lambda e: e.tensor_tensor(
        out=q1[:p, 0:Fv], in0=x_f32,
        in1=mx[:p, nb:2 * nb].unsqueeze(2).broadcast_to([p, nb, 16]),
        op=ALU.mult), [t_r6], "q.xn")
    t_eb = vop(lambda e: e.tensor_scalar(
        out=q2[:p, 0:Fv].bitcast(I32), in0=q1[:p, 0:Fv].bitcast(I32),
        scalar1=0x7F800000, scalar2=None, op0=ALU.bitwise_and), [t_xn], "q.eb")
    t_ebc = vop(lambda e: e.tensor_scalar(
        out=q2[:p, 0:Fv].bitcast(I32), in0=q2[:p, 0:Fv].bitcast(I32),
        scalar1=0x3F800000, scalar2=0x40800000, op0=ALU.max, op1=ALU.min),
        [t_eb], "q.ebc")
    t_rs = vop(lambda e: e.tensor_scalar(
        out=q3[:p, 0:Fv].bitcast(I32), in0=q2[:p, 0:Fv].bitcast(I32),
        scalar1=0x7F800000, scalar2=None, op0=ALU.bitwise_xor),
        [t_ebc], "q.rstep")
    t_st = vop(lambda e: e.tensor_scalar(
        out=q2[:p, 0:Fv], in0=q2[:p, 0:Fv], scalar1=0.5, scalar2=None,
        op0=ALU.mult), [t_ebc], "q.step")
    t_t = vop(lambda e: e.tensor_tensor(
        out=q1[:p, 0:Fv], in0=q1[:p, 0:Fv], in1=q3[:p, 0:Fv], op=ALU.mult),
        [t_rs], "q.t")
    t_rr = vop(lambda e: e.tensor_scalar(
        out=q1[:p, 0:Fv], in0=q1[:p, 0:Fv], scalar1=MAGIC1, scalar2=MAGIC1,
        op0=ALU.add, op1=ALU.subtract), [t_t], "q.round")
    t_q = vop(lambda e: e.tensor_tensor(
        out=q1[:p, 0:Fv], in0=q1[:p, 0:Fv], in1=q2[:p, 0:Fv], op=ALU.mult),
        [t_rr, t_st], "q.q")
    t_out = vop(lambda e: e.tensor_tensor(
        out=out_bf16, in0=q1[:p, 0:Fv],
        in1=mx[:p, 2 * nb:3 * nb].unsqueeze(2).broadcast_to([p, nb, 16]),
        op=ALU.mult), [t_q, t_s6], "q.out")
    return t_mx, t_out


def ln_ops(P, x_f32, xn_out, stats, sq, name, consts=None, out_extra_dep=()):
    t_ms = P.op("vector", lambda e: e.tensor_reduce(
        out=stats[:, 0:1], in_=x_f32, axis=AX.X, op=ALU.add), [], (),
        "c", name + ".msum")
    t_m = P.op("vector", lambda e: e.tensor_scalar(
        out=stats[:, 0:1], in0=stats[:, 0:1], scalar1=1.0 / DM, scalar2=None,
        op0=ALU.mult), [t_ms], (), "c", name + ".m")
    t_sq = P.op("scalar", lambda e: e.activation(
        out=sq[:, :], in_=x_f32, func=AF.Square), [], (), "c", name + ".sq")
    t_ss = P.op("vector", lambda e: e.tensor_reduce(
        out=stats[:, 1:2], in_=sq[:, :], axis=AX.X, op=ALU.add), [t_sq], (),
        "c", name + ".ssum")
    t_s2 = P.op("vector", lambda e: e.tensor_scalar(
        out=stats[:, 1:2], in0=stats[:, 1:2], scalar1=1.0 / DM, scalar2=None,
        op0=ALU.mult), [t_ss], (), "c", name + ".ess")
    t_mm = P.op("vector", lambda e: e.tensor_tensor(
        out=stats[:, 2:3], in0=stats[:, 0:1], in1=stats[:, 0:1], op=ALU.mult),
        [t_m], (), "c", name + ".m2")
    t_var = P.op("vector", lambda e: e.tensor_tensor(
        out=stats[:, 1:2], in0=stats[:, 1:2], in1=stats[:, 2:3],
        op=ALU.subtract), [t_s2, t_mm], (), "c", name + ".var")
    t_ln = P.op("scalar", lambda e: e.activation(
        out=stats[:, 2:3], in_=stats[:, 1:2], func=AF.Ln, bias=consts["eps"]),
        [t_var] + consts["op"], (), "c", name + ".ln")
    t_rstd = P.op("scalar", lambda e: e.activation(
        out=stats[:, 2:3], in_=stats[:, 2:3], func=AF.Exp, scale=consts["mhalf"]),
        [t_ln], (), "c", name + ".rstd")
    t_xn = P.op("vector", lambda e: e.tensor_scalar(
        out=xn_out, in0=x_f32, scalar1=stats[:, 0:1], scalar2=stats[:, 2:3],
        op0=ALU.subtract, op1=ALU.mult),
        [t_rstd, t_m] + list(out_extra_dep), (), "c", name + ".xn")
    return [t_ms, t_sq], t_xn


def build(upto=2, dbg=(), skip_wq=False):
    nc = bass.Bass("TRN2", target_bir_lowering=False)
    dbg = set(dbg)
    bank = SemBank(nc)
    P = Prog()

    def inp(name, shape, dt=F32):
        return nc.dram_tensor(name, shape, dt, kind="ExternalInput").ap()

    x_t = inp("x_t", [TOK, DM])
    w_inT = inp("w_inT", [DM, 2 * DS])
    dconv = inp("dconv", [128, 16 * 128])
    bconv_t = inp("bconv_t", [128, 4])
    w_xpT = inp("w_xpT", [DS, NE])
    w_dtT = inp("w_dtT", [64, DS])
    bdt_t = inp("bdt_t", [128, 4])
    alog_t = inp("alog_t", [128, 4 * NST])
    d_t = inp("d_t", [128, 4])
    w_outT = inp("w_outT", [DS, DM])
    wg_sh = inp("wg_sh", [IFS, DM])
    wu_sh = inp("wu_sh", [IFS, DM])
    wd_sh = inp("wd_sh", [WDS, IFF])
    ident = inp("ident", [128, 128], BF16)
    identf = inp("identf", [128, 128], F32)

    o_y = nc.dram_tensor("o_y", [TOK, DM], F32, kind="ExternalOutput").ap()

    b_xnj = [nc.dram_tensor(f"b_xn{j}", [DM, 512], BF16) for j in range(4)]
    b_xn_gj = [nc.dram_tensor(f"b_xn_g{j}", [4 * DM, 512], BF16) for j in range(4)]
    b_xdp = nc.dram_tensor("b_xdp", [NE, LB], F32)
    b_xdf = nc.dram_tensor("b_xdf", [NE, LB], F32)
    b_u = nc.dram_tensor("b_u", [128, 4 * LB], F32)
    b_sres = nc.dram_tensor("b_sres", [128, 4 * LB], F32)
    b_wopj = [nc.dram_tensor(f"b_wop{j}", [4 * DM, 512], F32) for j in range(4)]
    b_rsj = [nc.dram_tensor(f"b_rs{j}", [DM, 512], F32) for j in range(4)]
    b_res = nc.dram_tensor("b_res", [TOK, DM], F32)
    b_hq = nc.dram_tensor("b_hq", [IFF, TOK], BF16)
    b_wqg = nc.dram_tensor("b_wqg", [DM, IFS], BF16)
    b_wqu = nc.dram_tensor("b_wqu", [DM, IFS], BF16)
    b_wqd = nc.dram_tensor("b_wqd", [IFF, WDS], BF16)
    b_wqg_g = nc.dram_tensor("b_wqg_g", [8 * DM, IFS], BF16)
    b_wqu_g = nc.dram_tensor("b_wqu_g", [8 * DM, IFS], BF16)
    b_wqd_g = nc.dram_tensor("b_wqd_g", [8 * IFF, WDS], BF16)

    taps = {}

    def tap(name, src_ap, shape, dt, deps):
        if name not in dbg:
            return
        t = nc.dram_tensor("tap_" + name, shape, dt, kind="ExternalOutput").ap()
        taps[name] = P.dma("sync", lambda e, s=src_ap, t=t: e.dma_start(out=t, in_=s),
                           deps, "tap." + name)

    sbt = nc.alloc_sbuf_tensor
    ARN_COLS = 51200
    arena = sbt("arena", [128, ARN_COLS], F32)
    arn_f = arena.ap()
    arn_b = arena.ap().bitcast(BF16)

    class Al:
        def __init__(self, start=0):
            self.c = start

        def f(self, cols):
            off = self.c
            self.c += cols
            assert self.c <= ARN_COLS, f"arena overflow {self.c}"
            return arn_f[:, off:off + cols]

        def b(self, cols):
            off = self.c
            self.c += (cols + 1) // 2
            assert self.c <= ARN_COLS, f"arena overflow {self.c}"
            return arn_b[:, 2 * off:2 * off + cols]

    s_id = sbt("s_id", [128, 128], BF16)
    s_idf = sbt("s_idf", [128, 128], F32)
    s_small = sbt("s_small", [128, 256], F32)
    negA = s_small.ap()[:, 0:64]
    bdt = s_small.ap()[:, 64:68]
    dcol = s_small.ap()[:, 68:72]
    carry = s_small.ap()[:, 72:136]
    stats = s_small.ap()[:, 140:160]
    c_eps = s_small.ap()[:, 160:161]
    c_mhalf = s_small.ap()[:, 161:162]
    cst0 = P.op("vector", lambda e: e.memset(c_eps, 1e-5), (), (), "c", "cst.eps")
    cst1 = P.op("vector", lambda e: e.memset(c_mhalf, -0.5), (), (), "c", "cst.mhalf")
    consts = {"eps": c_eps, "mhalf": c_mhalf, "op": [cst0, cst1]}

    p_a = nc.alloc_psum_tensor("p_a", [128, 512], F32)
    p_b = nc.alloc_psum_tensor("p_b", [128, 512], F32)
    p_c = nc.alloc_psum_tensor("p_c", [128, 512], F32)
    p_d = nc.alloc_psum_tensor("p_d", [128, 512], F32)
    p_e = nc.alloc_psum_tensor("p_e", [128, 1024], F32)
    p_t1 = nc.alloc_psum_tensor("p_t1", [128, 128], F32)
    p_t2 = nc.alloc_psum_tensor("p_t2", [128, 128], F32)
    pt = [p_t1.ap(), p_t2.ap()]
    ptb = [p_t1.ap().bitcast(BF16)[:, 0:128], p_t2.ap().bitcast(BF16)[:, 0:128]]

    xblock = {}
    ld_id = P.dma("sync", lambda e: e.dma_start(out=s_id.ap(), in_=ident), (), "ld.id")
    ld_idf = P.dma("sync", lambda e: e.dma_start(out=s_idf.ap(), in_=identf), (), "ld.idf")

    al = Al()
    s_w = al.f(IFF)
    s_wq = al.b(IFF)
    s_wt = al.b(22 * 128)
    scr = dict(mx=al.f(3 * (IFF // 16)), q1=al.f(IFF), q2=al.f(IFF), q3=al.f(IFF))
    wq_al_end = al.c

    wq_jobs = []
    if skip_wq:
        wq_jobs = []
    for r0 in range(0, IFS, 128) if not skip_wq else []:
        rows = min(128, IFS - r0)
        wq_jobs.append((wg_sh[r0:r0 + rows, :], rows, DM, b_wqg, r0))
        wq_jobs.append((wu_sh[r0:r0 + rows, :], rows, DM, b_wqu, r0))
    wq_jobs.append((wd_sh[0:WDS, :], WDS, IFF, b_wqd, 0))

    prev_store = None
    prev_tr = {}
    wq_stores = []
    for i, (src, rows, cols, dst, r0) in enumerate(wq_jobs):
        deps = [prev_store] if prev_store else []
        t_ld = P.dma("sync", lambda e, s=src, r=rows, c=cols:
                     e.dma_start(out=s_w[0:r, 0:c], in_=s), deps, f"wq.ld{i}")
        q0, q1o = quant_ops(P, None, s_w[0:rows, 0:cols], s_wq[0:rows, 0:cols], scr)
        q0.deps.append(t_ld)
        nk = cols // 128
        evs = []
        for kt in range(nk):
            par = kt % 2
            tdeps = [q1o, ld_id]
            if par in prev_tr:
                tdeps.append(prev_tr[par][1])
            t_tr = P.op("tensor", lambda e, k=kt, p=par, r=rows:
                        e.transpose(ptb[p][0:128, 0:r], s_wq[0:r, k * 128:(k + 1) * 128],
                                    s_id.ap()[0:r, 0:r]),
                        tdeps, (), "c", f"wq.tr{i}.{kt}")
            t_ev = P.op("vector", lambda e, k=kt, p=par, r=rows:
                        e.tensor_copy(s_wt[:, k * 128:k * 128 + r], ptb[p][0:128, 0:r]),
                        [t_tr], (), "c", f"wq.ev{i}.{kt}")
            prev_tr[par] = (t_tr, t_ev)
            xblock[("pt", par)] = t_ev
            evs.append(t_ev)
        t_st = P.dma("sync", lambda e, d=dst, r0_=r0, r=rows, n=nk:
                     e.dma_start(
                         out=d.ap()[0:n * 128, r0_:r0_ + r]
                             .rearrange("(k p) c -> p k c", p=128),
                         in_=s_wt.rearrange("p (k c) -> p k c", c=128)[:, 0:n, 0:r]),
                     evs, f"wq.st{i}")
        wq_stores.append(t_st)
        prev_store = t_st

    if skip_wq:
        class _F: pass
        cc_wg = cc_wu = cc_wd = None
    cc_wg = cc_wg if skip_wq else P.cc(lambda e: e.collective_compute(
        "AllGather", ALU.bypass, replica_groups=WGROUP,
        ins=[b_wqg.ap().opt()], outs=[b_wqg_g.ap().opt()]), wq_stores, "cc.wg")
    cc_wu = cc_wu if skip_wq else P.cc(lambda e: e.collective_compute(
        "AllGather", ALU.bypass, replica_groups=WGROUP,
        ins=[b_wqu.ap().opt()], outs=[b_wqu_g.ap().opt()]), wq_stores, "cc.wu")
    cc_wd = cc_wd if skip_wq else P.cc(lambda e: e.collective_compute(
        "AllGather", ALU.bypass, replica_groups=WGROUP,
        ins=[b_wqd.ap().opt()], outs=[b_wqd_g.ap().opt()]), wq_stores, "cc.wd")
    tap("wqg", b_wqg.ap(), [DM, IFS], BF16, [cc_wg])
    tap("wqd", b_wqd.ap(), [IFF, WDS], BF16, [cc_wd])

    al = Al(wq_al_end)
    s_x0 = [al.f(DM), al.f(DM)]
    s_sq = al.f(DM)
    s_xn = al.b(DM)
    s_xnT = al.b(8 * 128)
    ph0_end = al.c

    last_ln = None
    last_ev_x = {}
    xn_stores = []
    for tt in range(16):
        par = tt % 2
        deps = []
        if tt >= 2:
            deps.append(last_ln[tt - 2])
        t_ld = P.dma("sync", lambda e, t=tt, p=par:
                     e.dma_start(out=s_x0[p], in_=x_t[t * 128:(t + 1) * 128, :]),
                     deps, f"ph0.ldx{tt}", stream=f"ph0.ldx{tt % 2}p")
        f_ops, t_xn = ln_ops(P, s_x0[par], s_xn, stats, s_sq, f"ph0.ln{tt}", consts)
        for f in f_ops:
            f.deps.append(t_ld)
        if tt > 0:
            f_ops[0].deps.append(last_tr_done)
            f_ops[1].deps.append(last_ln[tt - 1])
        if last_ln is None:
            last_ln = {}
        last_ln[tt] = t_xn
        evs = []
        for kt in range(8):
            par2 = kt % 2
            tdeps = [t_xn, ld_id]
            if (tt, kt) != (0, 0) and par2 in last_ev_x:
                tdeps.append(last_ev_x[par2])
            t_tr = P.op("tensor", lambda e, k=kt, p=par2:
                        e.transpose(ptb[p], s_xn[:, k * 128:(k + 1) * 128], s_id.ap()),
                        tdeps, (), "c", f"ph0.tr{tt}.{kt}")
            evdeps = [t_tr]
            if tt > 0 and kt == 0:
                evdeps.append(xn_stores[tt - 1])
            t_ev = P.op("vector", lambda e, k=kt, p=par2:
                        e.tensor_copy(s_xnT[:, k * 128:(k + 1) * 128], ptb[p]),
                        evdeps, (), "c", f"ph0.ev{tt}.{kt}")
            last_ev_x[par2] = t_ev
            xblock[("pt", par2)] = t_ev
            evs.append(t_ev)
        last_tr_done = evs[-1]
        t_st = P.dma("sync", lambda e, t=tt:
                     e.dma_start(
                         out=b_xnj[t // 4].ap()[:, (t % 4) * 128:(t % 4 + 1) * 128]
                             .rearrange("(k p) c -> p k c", p=128),
                         in_=s_xnT.rearrange("p (k c) -> p k c", c=128)),
                     evs, f"ph0.st{tt}")
        xn_stores.append(t_st)

    cc_xns = []
    for j in range(4):
        cc_xns.append(P.cc(lambda e, j_=j: e.collective_compute(
            "AllGather", ALU.bypass, replica_groups=GROUPS,
            ins=[b_xnj[j_].ap().opt()], outs=[b_xn_gj[j_].ap().opt()]),
            xn_stores[4 * j:4 * j + 4], f"cc.xn{j}"))
    cc_xn = cc_xns[-1]
    if upto < 1:
        P.emit(nc, bank, final=list(taps.values()) + cc_xns + [cc_wg, cc_wu, cc_wd])
        return nc, taps

    al = Al(ph0_end)
    s_rhs = [al.b(8 * L1C), al.b(8 * L1C)]
    s_winT = al.b(8 * 1024)
    s_dconv = al.b(16 * 128)
    s_wxpT = al.b(4 * NE)
    s_xc = al.b(4 * (L1C + 3))
    s_u1 = al.f(4 * L1C)
    s_ub = al.b(4 * L1C)
    s_sr1 = al.f(4 * L1C)
    s_xdps = al.f(L1C)
    s_bconv = al.f(4)
    ph1_end = al.c
    assert ph1_end <= ARN_COLS

    ld_win = P.dma("gpsimd", lambda e: e.dma_start(
        out=s_winT.rearrange("p (k m) -> p k m", m=1024),
        in_=w_inT.rearrange("(k p) m -> p k m", p=128)),
        xn_stores, "ph1.ldwin")
    ld_dc = P.dma("gpsimd", lambda e: e.dma_start(out=s_dconv, in_=dconv),
                  xn_stores, "ph1.lddc")
    ld_wxp = P.dma("gpsimd", lambda e: e.dma_start(
        out=s_wxpT.rearrange("p (k m) -> p k m", m=NE),
        in_=w_xpT.rearrange("(k p) m -> p k m", p=128)),
        xn_stores, "ph1.ldwxp")
    ld_bc = P.dma("sync", lambda e: e.dma_start(out=s_bconv, in_=bconv_t),
                  xn_stores, "ph1.ldbc")
    t_hz = P.op("vector", lambda e: e.memset(s_xc, 0.0), xn_stores, (), "c", "ph1.hz")

    prev = dict()
    xdp_stores = []
    u_stores = []
    sr_stores = []
    for c1 in range(N1C):
        par = c1 % 2
        r, off = c1 // 4, (c1 % 4) * L1C
        deps = [cc_xns[c1 % 4]]
        if c1 >= 2:
            deps.append(prev[("mm_last", (c1 - 2) % 2)])
        t_ldr = P.dma("sync", lambda e, r_=r, j_=c1 % 4, p=par: e.dma_start(
            out=s_rhs[p].rearrange("p (k c) -> p k c", c=L1C),
            in_=b_xn_gj[j_].ap()[r_ * DM:(r_ + 1) * DM, :]
                .rearrange("(k p) c -> p k c", p=128)),
            deps, f"ph1.ldr{c1}", stream=f"ph1.ldr{c1 % 2}p")
        for m in range(8):
            pp = [p_a, p_b, p_c, p_d][m % 4]
            mdeps = [t_ldr, ld_win]
            key = ("winpsum", m % 4)
            if key in prev:
                mdeps.append(prev[key])
            mm = None
            for k in range(8):
                mm = P.op("tensor", lambda e, k_=k, m_=m, p_=pp, pr=par: e.matmul(
                    p_.ap()[:, 0:L1C],
                    s_winT[:, k_ * 1024 + m_ * 128:k_ * 1024 + (m_ + 1) * 128],
                    s_rhs[pr][:, k_ * L1C:(k_ + 1) * L1C],
                    start=(k_ == 0), stop=(k_ == 7)),
                    mdeps if k == 0 else [], (), "c", f"ph1.mm{c1}.{m}.{k}")
            if m == 7:
                prev[("mm_last", par)] = mm
            if m < 4:
                hdeps = []
                if c1 > 0:
                    hdeps = [prev[("xc_ev", m)]]
                    if ("conv_mm", m) in prev:
                        hdeps.append(prev[("conv_mm", m)])
                    t_h = P.op("vector", lambda e, m_=m: e.tensor_copy(
                        s_xc[:, m_ * 515:m_ * 515 + 3],
                        s_xc[:, m_ * 515 + 512:m_ * 515 + 515]),
                        hdeps, (), "c", f"ph1.halo{c1}.{m}")
                else:
                    t_h = t_hz
                t_ev = P.op("vector", lambda e, m_=m, p_=pp: e.tensor_copy(
                    s_xc[:, m_ * 515 + 3:m_ * 515 + 515], p_.ap()[:, 0:L1C]),
                    [mm, t_h], (), "c", f"ph1.xcev{c1}.{m}")
                prev[("xc_ev", m)] = t_ev
                prev[("winpsum", m % 4)] = t_ev
                prev[("xc_ready", m)] = t_ev
            else:
                mprev = m - 4
                sdeps = [mm]
                if ("sr_store",) in prev and c1 > 0 and mprev == 0:
                    sdeps.append(prev[("sr_store",)])
                t_sr = P.op("scalar", lambda e, m_=mprev, p_=pp: e.activation(
                    out=s_sr1[:, m_ * L1C:(m_ + 1) * L1C], in_=p_.ap()[:, 0:L1C],
                    func=AF.Silu), sdeps, (), "c", f"ph1.sr{c1}.{m}")
                prev[("winpsum", m % 4)] = t_sr
                prev[("sr", mprev)] = t_sr
        t_sst = P.dma("sync", lambda e, o=c1 * L1C: e.dma_start(
            out=b_sres.ap().rearrange("p (t c) -> p t c", t=4)[:, :, o:o + L1C],
            in_=s_sr1.rearrange("p (t c) -> p t c", t=4)),
            [prev[("sr", i)] for i in range(4)], f"ph1.srst{c1}")
        prev[("sr_store",)] = t_sst
        sr_stores.append(t_sst)
        for m in range(4):
            po_ = (m % 2) * 512
            cdeps = [prev[("xc_ready", m)], ld_dc]
            key = ("convpsum", m % 2)
            if key in prev:
                cdeps.append(prev[key])
            if m < 2 and ("xdp_ev",) in prev:
                cdeps.append(prev[("xdp_ev",)])
            mm = None
            for k in range(4):
                mm = P.op("tensor", lambda e, k_=k, m_=m, po=po_: e.matmul(
                    p_e.ap()[:, po:po + L1C],
                    s_dconv[:, (m_ * 4 + k_) * 128:(m_ * 4 + k_ + 1) * 128],
                    s_xc[:, m_ * 515 + k_:m_ * 515 + k_ + L1C],
                    start=(k_ == 0), stop=(k_ == 3)),
                    cdeps if k == 0 else [], (), "c", f"ph1.cv{c1}.{m}.{k}")
            prev[("conv_mm", m)] = mm
            udeps = [mm]
            if c1 > 0 and m == 0 and ("u_store",) in prev:
                udeps.append(prev[("u_store",)])
            t_us = P.op("scalar", lambda e, m_=m, po=po_: e.activation(
                out=s_u1[:, m_ * L1C:(m_ + 1) * L1C], in_=p_e.ap()[:, po:po + L1C],
                func=AF.Silu, bias=s_bconv[:, m_:m_ + 1]),
                udeps + [ld_bc], (), "c", f"ph1.us{c1}.{m}")
            ubdeps = [t_us]
            if ("xdbl_mm",) in prev:
                ubdeps.append(prev[("xdbl_mm",)])
            t_ub = P.op("vector", lambda e, m_=m: e.tensor_copy(
                s_ub[:, m_ * L1C:(m_ + 1) * L1C],
                s_u1[:, m_ * L1C:(m_ + 1) * L1C]),
                ubdeps, (), "c", f"ph1.ub{c1}.{m}")
            prev[("convpsum", m % 2)] = t_us
            prev[("ub", m)] = t_ub
            prev[("us", m)] = t_us
        t_ust = P.dma("sync", lambda e, o=c1 * L1C: e.dma_start(
            out=b_u.ap().rearrange("p (t c) -> p t c", t=4)[:, :, o:o + L1C],
            in_=s_u1.rearrange("p (t c) -> p t c", t=4)),
            [prev[("us", i)] for i in range(4)], f"ph1.ust{c1}")
        prev[("u_store",)] = t_ust
        u_stores.append(t_ust)
        xdeps = [prev[("ub", i)] for i in range(4)] + [ld_wxp]
        xdeps += [prev[("us", i)] for i in range(4)]
        if ("xdp_ev",) in prev:
            xdeps.append(prev[("xdp_ev",)])
        mm = None
        for k in range(4):
            mm = P.op("tensor", lambda e, k_=k: e.matmul(
                p_e.ap()[0:NE, 0:L1C],
                s_wxpT[:, k_ * NE:(k_ + 1) * NE],
                s_ub[:, k_ * L1C:(k_ + 1) * L1C],
                start=(k_ == 0), stop=(k_ == 3)),
                xdeps if k == 0 else [], (), "c", f"ph1.xd{c1}.{k}")
        prev[("xdbl_mm",)] = mm
        xst_dep = []
        if ("xdp_st",) in prev:
            xst_dep.append(prev[("xdp_st",)])
        t_xe = P.op("scalar", lambda e: e.activation(
            out=s_xdps[0:NE, :], in_=p_e.ap()[0:NE, 0:L1C], func=AF.Copy),
            [mm] + xst_dep, (), "c", f"ph1.xde{c1}")
        prev[("xdp_ev",)] = t_xe
        t_xst = P.dma("sync", lambda e, o=c1 * L1C: e.dma_start(
            out=b_xdp[0:NE, o:o + L1C], in_=s_xdps[0:NE, :]),
            [t_xe], f"ph1.xdst{c1}")
        prev[("xdp_st",)] = t_xst
        xdp_stores.append(t_xst)

    cc_xd = P.cc(lambda e: e.collective_compute(
        "AllReduce", ALU.add, replica_groups=GROUPS,
        ins=[b_xdp.ap().opt()], outs=[b_xdf.ap().opt()]), xdp_stores, "cc.xd")
    tap("xdf", b_xdf.ap(), [NE, LB], F32, [cc_xd])
    tap("u", b_u.ap(), [128, 4 * LB], F32, [u_stores[-1]] + u_stores[:-1])
    tap("sres", b_sres.ap(), [128, 4 * LB], F32, [sr_stores[-1]] + sr_stores[:-1])

    if upto < 2:
        P.emit(nc, bank, final=list(taps.values()) + [cc_xd] +
               ([] if skip_wq else [cc_wg, cc_wu, cc_wd]))
        return nc, taps

    P.emit(nc, bank)
    nc.all_engine_barrier()
    P2 = Prog()
    P = P2

    gate_deps = [cc_xd, u_stores[-1], sr_stores[-1], xdp_stores[-1],
                 xn_stores[-1]]
    if wq_stores:
        gate_deps += [wq_stores[-1]]
    ph2_gate = P.op("sync", lambda e: e.nop(), gate_deps, (), "c", "ph2.gate")

    al = Al()
    LCP = LC + 1
    sX = [al.f(4 * LCP), al.f(4 * LCP)]
    sY = [al.f(4 * LCP), al.f(4 * LCP)]
    sHC = [al.f(4 * LC), al.f(4 * LC)]
    sB = [al.f(4 * LC), al.f(4 * LC)]
    sC = [al.f(4 * LC), al.f(4 * LC)]
    sU2 = [al.f(4 * LC)]
    sS2 = [al.f(4 * LC)]
    sDel = [al.f(4 * LC), al.f(4 * LC)]
    sDu = [al.f(4 * LC), al.f(4 * LC)]
    sDt = al.b(4096)
    sR4 = [al.f(4 * LC) for _ in range(4)]
    sYg = al.b(4 * LC)
    sWo = [al.f(LC), al.f(LC)]
    sT0 = al.f(LC)
    sXb = al.f(LC)
    sLn = al.f(LC)
    sG1 = sT0
    sG2 = sXb
    sWoT = al.b(4 * 1024)
    sWdtT = al.b(512)
    sAl = s_small.ap()[:, 162:226]
    assert al.c <= ARN_COLS, al.c

    ld_dt0 = P.dma("gpsimd", lambda e: e.dma_start(
        out=sDt[0:64, :], in_=b_xdf[0:64, 0:4096]), [ph2_gate], "p2.lddta")
    ld_dt1 = P.dma("gpsimd", lambda e: e.dma_start(
        out=sDt[64:128, :], in_=b_xdf[0:64, 4096:8192]), [ph2_gate], "p2.lddtb")
    ld_al = P.dma("sync", lambda e: e.dma_start(out=sAl, in_=alog_t),
                  [ph2_gate], "p2.ldal")
    ld_bdt = P.dma("sync", lambda e: e.dma_start(
        out=s_small.ap()[:, 64:68], in_=bdt_t), [ph2_gate], "p2.ldbdt")
    ld_d = P.dma("sync", lambda e: e.dma_start(
        out=s_small.ap()[:, 68:72], in_=d_t), [ph2_gate], "p2.ldd")
    ld_wo = P.dma("gpsimd", lambda e: e.dma_start(
        out=sWoT.rearrange("p (k m) -> p k m", m=1024),
        in_=w_outT.rearrange("(k p) m -> p k m", p=128)), [ph2_gate], "p2.ldwo")
    ld_wdt0 = P.dma("gpsimd", lambda e: e.dma_start(
        out=sWdtT[0:64, :], in_=w_dtT), [ph2_gate], "p2.ldwdt")
    ld_wdt1 = P.dma("gpsimd", lambda e: e.dma_start(
        out=sWdtT[64:128, :], in_=w_dtT), [ph2_gate], "p2.ldwdtb")
    zms = []
    for par in range(2):
        zms.append(P.op("vector", lambda e, p=par: e.memset(
            sX[p].rearrange("p (n c) -> p n c", c=LCP)[:, :, 0], 0.0),
            [ph2_gate], (), "c", f"p2.zx{par}"))
    zc = P.op("vector", lambda e: e.memset(carry[:, 0:64], 0.0),
              [ph2_gate], (), "c", "p2.zc")
    t_eA = P.op("scalar", lambda e: e.activation(
        out=negA, in_=sAl[:, 0:64], func=AF.Exp), [ld_al], (), "c", "p2.eA")
    t_nA = P.op("vector", lambda e: e.tensor_scalar(
        out=negA, in0=negA, scalar1=-1.0, scalar2=None, op0=ALU.mult),
        [t_eA], (), "c", "p2.nA")

    prev = {}
    wop_stores = []
    stage_idx = 0
    for cc in range(NCH):
        lo = cc * LC
        r, off = cc // 4, (cc % 4) * LC
        ppar = 0
        udeps = [ph2_gate]
        for k in (("du", ppar), ("gate_u", ppar)):
            if k in prev:
                udeps.append(prev[k])
        t_ldu = P.dma("sync", lambda e, o=lo, p=ppar: e.dma_start(
            out=sU2[p].rearrange("p (t c) -> p t c", t=4),
            in_=b_u.ap().rearrange("p (t c) -> p t c", t=4)[:, :, o:o + LC]),
            udeps, f"p2.ldu{cc}", stream=f"p2.ldu{cc % 2}")
        sdeps = [ph2_gate]
        if ("sres_tt", ppar) in prev:
            sdeps.append(prev[("sres_tt", ppar)])
        t_lds = P.dma("sync", lambda e, o=lo, p=ppar: e.dma_start(
            out=sS2[p].rearrange("p (t c) -> p t c", t=4),
            in_=b_sres.ap().rearrange("p (t c) -> p t c", t=4)[:, :, o:o + LC]),
            sdeps, f"p2.lds{cc}", stream=f"p2.lds{cc % 2}")
        for t in range(4):
            pp = [p_a, p_b][t % 2]
            dhalf = 0 if cc < 8 else 64
            rhs_off = (cc % 8) * LC
            mdeps = [ld_dt0, ld_dt1, ld_wdt0, ld_wdt1]
            if ("dpsum", t % 2) in prev:
                mdeps.append(prev[("dpsum", t % 2)])
            mm = P.op("tensor", lambda e, t_=t, p_=pp, dh=dhalf, ro=rhs_off: e.matmul(
                p_.ap()[:, 0:LC],
                sWdtT[dh:dh + 64, t_ * 128:(t_ + 1) * 128],
                sDt[dh:dh + 64, ro:ro + LC],
                start=True, stop=True),
                mdeps, (), "c", f"p2.dmm{cc}.{t}")
            adeps = [mm, ld_bdt]
            if ("chain",) in prev:
                adeps.append(prev[("chain",)])
            if ("deltatile", t, cc % 2) in prev:
                adeps.append(prev[("deltatile", t, cc % 2)])
            if ("gates_done",) in prev:
                adeps.append(prev[("gates_done",)])
            a1 = P.op("scalar", lambda e, t_=t, p_=pp: e.activation(
                out=sT0[:, :], in_=p_.ap()[:, 0:LC], func=AF.Abs,
                bias=s_small.ap()[:, 64 + t_:65 + t_]), adeps, (), "c",
                f"p2.abs{cc}.{t}")
            a3 = P.op("scalar", lambda e: e.activation(
                out=sLn[:, :], in_=sT0[:, :], func=AF.Exp, scale=-1.0),
                [a1], (), "c", f"p2.exp{cc}.{t}")
            a4 = P.op("scalar", lambda e: e.activation(
                out=sLn[:, :], in_=sLn[:, :], func=AF.Ln, bias=1.0),
                [a3], (), "c", f"p2.ln{cc}.{t}")
            v1 = P.op("vector", lambda e, t_=t, p_=pp: e.tensor_scalar(
                out=sXb[:, :], in0=p_.ap()[:, 0:LC],
                scalar1=s_small.ap()[:, 64 + t_:65 + t_], scalar2=0.0,
                op0=ALU.add, op1=ALU.max),
                [a1], (), "c", f"p2.relu{cc}.{t}")
            prev[("dpsum", t % 2)] = v1
            v2 = P.op("vector", lambda e, t_=t, dp=cc % 2: e.tensor_tensor(
                out=sDel[dp][:, t_ * LC:(t_ + 1) * LC], in0=sXb[:, :],
                in1=sLn[:, :], op=ALU.add),
                [v1, a4], (), "c", f"p2.del{cc}.{t}")
            prev[("chain",)] = v2
            prev[("delta", t)] = v2
        dudeps = [prev[("delta", t)] for t in range(4)] + [t_ldu]
        if ("dbu_last",) in prev:
            dudeps.append(prev[("dbu_last",)])
        t_du = P.op("vector", lambda e, p=ppar, dp=cc % 2: e.tensor_tensor(
            out=sDu[dp][:, :], in0=sDel[dp][:, :], in1=sU2[p][:, :], op=ALU.mult),
            dudeps, (), "c", f"p2.du{cc}")
        prev[("du", ppar)] = t_du
        for q in range(4):
            qpar = q % 2
            bdeps = [cc_xd, ph2_gate]
            if ("dbu_q", qpar) in prev:
                bdeps.append(prev[("dbu_q", qpar)])
            t_ldb = P.dma("sync", lambda e, q_=q, o=lo, p=qpar: e.dma_start(
                out=sB[p].rearrange("p (n c) -> p n c", n=4),
                in_=b_xdf.ap()[64 + q_ * 4:68 + q_ * 4, o:o + LC]
                    .unsqueeze(0).partition_broadcast(128)),
                bdeps, f"p2.ldb{cc}.{q}", stream=f"p2.ldb{qpar}")
            cdeps = [cc_xd, ph2_gate]
            if ("hc_q", qpar) in prev:
                cdeps.append(prev[("hc_q", qpar)])
            t_ldc = P.dma("sync", lambda e, q_=q, o=lo, p=qpar: e.dma_start(
                out=sC[p].rearrange("p (n c) -> p n c", n=4),
                in_=b_xdf.ap()[80 + q_ * 4:84 + q_ * 4, o:o + LC]
                    .unsqueeze(0).partition_broadcast(128)),
                cdeps, f"p2.ldc{cc}.{q}", stream=f"p2.ldc{qpar}")
            for t in range(4):
                sp = stage_idx % 2
                adeps = [prev[("delta", t)], t_nA, zms[sp]]
                if ("scan_x", sp) in prev:
                    adeps.append(prev[("scan_x", sp)])
                da = None
                for n in range(4):
                    da = P.op("scalar", lambda e, t_=t, q_=q, n_=n, s=sp, dp=cc % 2: e.activation(
                        out=sX[s][:, n_ * LCP + 1:(n_ + 1) * LCP],
                        in_=sDel[dp][:, t_ * LC:(t_ + 1) * LC], func=AF.Exp,
                        scale=negA[:, t_ * 16 + q_ * 4 + n_:t_ * 16 + q_ * 4 + n_ + 1]),
                        adeps if n == 0 else [], (), "c", f"p2.dA{cc}.{q}.{t}.{n}")
                if q == 3:
                    prev[("deltatile", t, cc % 2)] = da
                budeps = [t_du, t_ldb]
                if ("y_hc", sp) in prev:
                    budeps.append(prev[("y_hc", sp)])
                if ("y_cp", sp) in prev:
                    budeps.append(prev[("y_cp", sp)])
                dbu = P.op("gpsimd", lambda e, t_=t, s=sp, p=qpar, dp=cc % 2: e.tensor_tensor(
                    out=sY[s].rearrange("p (n c) -> p n c", c=LCP)[:, :, 1:LCP],
                    in0=sDu[dp][:, t_ * LC:(t_ + 1) * LC].unsqueeze(1)
                        .broadcast_to([128, 4, LC]),
                    in1=sB[p].rearrange("p (n c) -> p n c", n=4), op=ALU.mult),
                    budeps, (), "c", f"p2.dbu{cc}.{q}.{t}")
                prev[("dbu_q", qpar)] = dbu
                prev[("dbu_last",)] = dbu
                cideps = [zc]
                if ("cpx", t, q) in prev:
                    cideps.append(prev[("cpx", t, q)])
                if ("y_hc", sp) in prev:
                    cideps.append(prev[("y_hc", sp)])
                if ("y_cp", sp) in prev:
                    cideps.append(prev[("y_cp", sp)])
                ci = P.op("gpsimd", lambda e, t_=t, q_=q, s=sp: e.tensor_copy(
                    sY[s].rearrange("p (n c) -> p n c", c=LCP)[:, :, 0],
                    carry[:, t_ * 16 + q_ * 4:t_ * 16 + q_ * 4 + 4]),
                    cideps, (), "c", f"p2.ci{cc}.{q}.{t}")
                scdeps = [da, dbu, ci]
                sc = P.op("vector", lambda e, s=sp: e.tensor_tensor_scan(
                    out=sY[s][:, :], data0=sX[s][:, :], data1=sY[s][:, :],
                    initial=0.0, op0=ALU.mult, op1=ALU.add),
                    scdeps, (), "c", f"p2.sc{cc}.{q}.{t}")
                prev[("scan_x", sp)] = sc
                hcdeps = [sc, t_ldc]
                if ("rd_hc", sp) in prev:
                    hcdeps.append(prev[("rd_hc", sp)])
                hc = P.op("vector",
                    lambda e, s=sp, p=qpar: e.tensor_tensor(
                    out=sHC[s][:, :],
                    in0=sY[s].rearrange("p (n c) -> p n c", c=LCP)[:, :, 1:LCP],
                    in1=sC[p].rearrange("p (n c) -> p n c", n=4), op=ALU.mult),
                    hcdeps, (), "c", f"p2.hc{cc}.{q}.{t}")
                cp = P.op("gpsimd", lambda e, t_=t, q_=q, s=sp: e.tensor_copy(
                    carry[:, t_ * 16 + q_ * 4:t_ * 16 + q_ * 4 + 4],
                    sY[s].rearrange("p (n c) -> p n c", c=LCP)[:, :, LCP - 1]),
                    [sc], (), "c", f"p2.cp{cc}.{q}.{t}")
                prev[("cpx", t, q)] = cp
                prev[("y_hc", sp)] = hc
                prev[("y_cp", sp)] = cp
                prev[("hc_q", qpar)] = hc
                rddeps = [hc]
                if ("rd_war", q, t) in prev:
                    rddeps.append(prev[("rd_war", q, t)])
                rd_spec = (rddeps, sp, q, t, cc)
                if ("rd_pending",) in prev:
                    prev[("rd_pending",)]()
                def mk_rd(spec=rd_spec):
                    deps, s_, q_, t_, c_ = spec
                    rd = P.op("vector", lambda e, s=s_, qq=q_, tq=t_: e.tensor_reduce(
                        out=sR4[qq][:, tq * LC:(tq + 1) * LC],
                        in_=sHC[s].rearrange("p (n c) -> p c n", n=4),
                        axis=AX.X, op=ALU.add),
                        deps, (), "c", f"p2.rd{c_}.{q_}.{t_}")
                    prev[("rd_hc", s_)] = rd
                    prev[("rd", t_, q_)] = rd
                    prev.pop(("rd_pending",), None)
                prev[("rd_pending",)] = mk_rd
                stage_idx += 1
        if ("rd_pending",) in prev:
            prev[("rd_pending",)]()
        for t in range(4):
            gdeps = [prev[("rd", t, q)] for q in range(4)] + [t_ldu, ld_d]
            if ("wout_mm",) in prev:
                gdeps.append(prev[("wout_mm",)])
            g1 = P.op("vector", lambda e, t_=t, p=ppar: e.scalar_tensor_tensor(
                out=sG1[:, :], in0=sU2[p][:, t_ * LC:(t_ + 1) * LC],
                scalar=s_small.ap()[:, 68 + t_:69 + t_],
                in1=sR4[0][:, t_ * LC:(t_ + 1) * LC],
                op0=ALU.mult, op1=ALU.add), gdeps, (), "c", f"p2.g1{cc}.{t}")
            g2 = P.op("vector", lambda e, t_=t: e.tensor_tensor(
                out=sG2[:, :], in0=sG1[:, :],
                in1=sR4[1][:, t_ * LC:(t_ + 1) * LC], op=ALU.add),
                [g1], (), "c", f"p2.g2{cc}.{t}")
            g3 = P.op("vector", lambda e, t_=t: e.tensor_tensor(
                out=sG1[:, :], in0=sG2[:, :],
                in1=sR4[2][:, t_ * LC:(t_ + 1) * LC], op=ALU.add),
                [g2], (), "c", f"p2.g3{cc}.{t}")
            g4 = P.op("vector", lambda e, t_=t: e.tensor_tensor(
                out=sG2[:, :], in0=sG1[:, :],
                in1=sR4[3][:, t_ * LC:(t_ + 1) * LC], op=ALU.add),
                [g3], (), "c", f"p2.g4{cc}.{t}")
            yg = P.op("vector", lambda e, t_=t, p=ppar: e.tensor_tensor(
                out=sYg[:, t_ * LC:(t_ + 1) * LC], in0=sG2[:, :],
                in1=sS2[p][:, t_ * LC:(t_ + 1) * LC], op=ALU.mult),
                [g4, t_lds], (), "c", f"p2.yg{cc}.{t}")
            prev[("gate_u", ppar)] = g1
            prev[("sres_tt", ppar)] = yg
            prev[("gates_done",)] = yg
            prev[("yg", t)] = yg
            for q in range(4):
                prev[("rd_war", q, t)] = g4 if q >= 2 else g2
        for m in range(8):
            pp = [p_c, p_d][m % 2]
            wdeps = [prev[("yg", t)] for t in range(4)] + [ld_wo]
            if ("wo_ev", m % 2) in prev:
                wdeps.append(prev[("wo_ev", m % 2)])
            mm = None
            for k in range(4):
                mm = P.op("tensor", lambda e, k_=k, m_=m, p_=pp: e.matmul(
                    p_.ap()[:, 0:LC],
                    sWoT[:, k_ * 1024 + m_ * 128:k_ * 1024 + (m_ + 1) * 128],
                    sYg[:, k_ * LC:(k_ + 1) * LC],
                    start=(k_ == 0), stop=(k_ == 3)),
                    wdeps if k == 0 else [], (), "c", f"p2.wmm{cc}.{m}.{k}")
            prev[("wout_mm",)] = mm
            evdeps = [mm]
            if ("wo_st", m % 2) in prev:
                evdeps.append(prev[("wo_st", m % 2)])
            ev = P.op("scalar", lambda e, m_=m, p_=pp: e.activation(
                out=sWo[m_ % 2][:, :], in_=p_.ap()[:, 0:LC], func=AF.Copy),
                evdeps, (), "c", f"p2.wev{cc}.{m}")
            prev[("wo_ev", m % 2)] = ev
            st_ = P.dma("sync", lambda e, m_=m, r_=r, j_=cc % 4: e.dma_start(
                out=b_wopj[j_].ap()[r_ * DM + m_ * 128:r_ * DM + (m_ + 1) * 128, :],
                in_=sWo[m_ % 2][:, :]),
                [ev], f"p2.wst{cc}.{m}", stream=f"p2.wst{m % 2}")
            prev[("wo_st", m % 2)] = st_
            wop_stores.append(st_)

    cc_rsj = []
    for j in range(4):
        jdeps = [wop_stores[8 * c + m] for c in range(j, NCH, 4) for m in range(8)]
        cc_rsj.append(P.cc(lambda e, j_=j: e.collective_compute(
            "ReduceScatter", ALU.add, replica_groups=GROUPS,
            ins=[b_wopj[j_].ap().opt()], outs=[b_rsj[j_].ap().opt()]),
            jdeps, f"cc.rs{j}"))
    cc_rs = cc_rsj[-1]

    if upto < 3:
        P.emit(nc, bank, final=list(taps.values()) + cc_rsj +
               ([] if skip_wq else [cc_wg, cc_wu, cc_wd]))
        return nc, taps

    P.emit(nc, bank)
    nc.all_engine_barrier()
    P3 = Prog()
    P = P3

    al = Al()
    sXqfm = al.b(16 * 1024)
    scr2 = dict(mx=al.f(3 * (IFF // 16)), q1=al.f(IFF), q2=al.f(IFF),
                q3=al.f(IFF))
    ph4_base = al.c
    sAt = [al.f(1024), al.f(1024)]
    sX3 = [al.f(1024), al.f(1024)]
    sRes = [al.f(1024), al.f(1024)]
    sSq3 = [al.f(1024), al.f(1024)]
    sXn2 = [al.f(1024), al.f(1024)]
    sXq = [al.b(1024), al.b(1024)]
    assert al.c <= ARN_COLS, al.c
    scr3 = [dict(mx=scr2["mx"][:, 0:192], q1=scr2["q1"][:, 0:1024],
                 q2=scr2["q2"][:, 0:1024], q3=scr2["q3"][:, 0:1024]),
            dict(mx=scr2["mx"][:, 192:384], q1=scr2["q1"][:, 1024:2048],
                 q2=scr2["q2"][:, 1024:2048], q3=scr2["q3"][:, 1024:2048])]
    stats3 = [s_small.ap()[:, 140:148], s_small.ap()[:, 148:156]]

    ph3_gate = P.op("sync", lambda e: e.nop(), [wop_stores[-1]], (),
                    "c", "ph3.gate")
    prev = {}
    res_stores = []

    def ph3_tile(tt):
        par = tt % 2
        adeps = [ph3_gate, cc_rsj[tt // 4]]
        if ("tr_last", par) in prev:
            adeps.append(prev[("tr_last", par)])
        t_lda = P.dma("sync", lambda e, t=tt, p=par: e.dma_start(
            out=sAt[p].rearrange("p (k c) -> p k c", c=128),
            in_=b_rsj[t // 4].ap()[:, (t % 4) * 128:(t % 4 + 1) * 128]
                .rearrange("(k p) c -> p k c", p=128)),
            adeps, f"p3.lda{tt}", stream=f"p3.lda{par}")
        xdeps = [ph3_gate]
        if ("resid", par) in prev:
            xdeps.append(prev[("resid", par)])
        t_ldx = P.dma("sync", lambda e, t=tt, p=par: e.dma_start(
            out=sX3[p], in_=x_t[t * 128:(t + 1) * 128, :]),
            xdeps, f"p3.ldx{tt}", stream=f"p3.ldx{par}")
        yield
        revs = []
        for kt in range(8):
            p2_ = kt % 2
            tdeps = [t_lda, ld_idf]
            if ("pt", p2_) in xblock:
                xblock[("pt", p2_)].sig = True
                tdeps.append(xblock[("pt", p2_)])
            t_tr = P.op("tensor", lambda e, k=kt, p=par, p_=p2_: e.transpose(
                pt[p_], sAt[p][:, k * 128:(k + 1) * 128], s_idf.ap()),
                tdeps, (), "c", f"p3.tr{tt}.{kt}")
            evdeps = [t_tr, t_ldx]
            if ("res_store", par) in prev and kt == 0:
                evdeps.append(prev[("res_store", par)])
            if ("ln_done", par) in prev and kt == 0:
                evdeps.append(prev[("ln_done", par)])
            t_ev = P.op("vector", lambda e, k=kt, p=par, p_=p2_: e.tensor_tensor(
                out=sRes[p][:, k * 128:(k + 1) * 128],
                in0=sX3[p][:, k * 128:(k + 1) * 128], in1=pt[p_], op=ALU.add),
                evdeps, (), "c", f"p3.ev{tt}.{kt}")
            xblock[("pt", p2_)] = t_ev
            revs.append(t_ev)
            if kt % 2 == 1:
                yield
        prev[("tr_last", par)] = revs[-1]
        t_res = revs[-1]
        prev[("resid", par)] = t_res
        t_rst = P.dma("sync", lambda e, t=tt, p=par: e.dma_start(
            out=b_res[t * 128:(t + 1) * 128, :], in_=sRes[p]),
            revs, f"p3.rst{tt}", stream=f"p3.rst{par}")
        res_stores.append(t_rst)
        prev[("res_store", par)] = t_rst
        yield
        f_ops, t_xn2 = ln_ops(P, sRes[par], sXn2[par], stats3[par], sSq3[par],
                              f"p3.ln{tt}", consts)
        for f in f_ops:
            f.deps += revs
            if ("ln_done", par) in prev:
                f.deps.append(prev[("ln_done", par)])
        yield
        q0, qo = quant_ops(P, None, sXn2[par], sXq[par], scr3[par])
        q0.deps.append(t_xn2)
        if ("xq_tr", par) in prev:
            qo.deps.append(prev[("xq_tr", par)])
        prev[("ln_done", par)] = t_xn2
        yield
        for kt in range(8):
            p2_ = kt % 2
            tdeps = [qo, ld_id]
            if ("pt", p2_) in xblock:
                tdeps.append(xblock[("pt", p2_)])
            t_tr = P.op("tensor", lambda e, k=kt, p=par, p_=p2_: e.transpose(
                ptb[p_], sXq[p][:, k * 128:(k + 1) * 128], s_id.ap()),
                tdeps, (), "c", f"p3.qtr{tt}.{kt}")
            t_ev = P.op("vector", lambda e, k=kt, t_=tt, p_=p2_: e.tensor_copy(
                sXqfm[:, (k * 16 + t_) * 128:(k * 16 + t_) * 128 + 128],
                ptb[p_]), [t_tr], (), "c", f"p3.qev{tt}.{kt}")
            xblock[("pt", p2_)] = t_ev
            if kt == 7:
                prev[("xq_tr", par)] = t_ev
            if kt % 2 == 1:
                yield
        prev[("xqdone", tt)] = prev[("xq_tr", par)]

    for base in range(0, 16, 2):
        g0, g1 = ph3_tile(base), ph3_tile(base + 1)
        done0 = done1 = False
        while not (done0 and done1):
            if not done0:
                try:
                    next(g0)
                except StopIteration:
                    done0 = True
            if not done1:
                try:
                    next(g1)
                except StopIteration:
                    done1 = True

    for t in range(16):
        prev[("xqdone", t)].sig = True
    P.emit(nc, bank)
    nc.all_engine_barrier()
    P4 = Prog()
    P = P4

    al = Al(ph4_base)
    sWqg = al.b(22528)
    sWqu = al.b(22528)
    sH = al.f(IFF)
    sSg = al.f(1024)
    sHq = [al.b(IFF), al.b(IFF)]
    sHqT = [al.b(22 * 128), al.b(22 * 128)]
    assert al.c <= ARN_COLS, al.c

    g4deps = [prev[("xqdone", t)] for t in range(16)] + res_stores[-2:]
    if not skip_wq:
        g4deps += [cc_wg, cc_wu]
    ph4_gate = P.op("sync", lambda e: e.nop(), g4deps, (), "c", "ph4.gate")
    ld_wg = ld_wu = None
    for rr in range(8):
        ld_wg = P.dma("sync", lambda e, r_=rr: e.dma_start(
            out=sWqg.rearrange("p (k s) -> p k s", k=8)[:, :, r_ * IFS:(r_ + 1) * IFS],
            in_=b_wqg_g.ap()[r_ * DM:(r_ + 1) * DM, :]
                .rearrange("(k p) s -> p k s", p=128)),
            [ph4_gate], f"p4.ldwg{rr}", stream="p4.ldwg")
        ld_wu = P.dma("sync", lambda e, r_=rr: e.dma_start(
            out=sWqu.rearrange("p (k s) -> p k s", k=8)[:, :, r_ * IFS:(r_ + 1) * IFS],
            in_=b_wqu_g.ap()[r_ * DM:(r_ + 1) * DM, :]
                .rearrange("(k p) s -> p k s", p=128)),
            [ph4_gate], f"p4.ldwu{rr}", stream="p4.ldwu")

    NCC = [(0, 512), (512, 512), (1024, 512), (1536, 512), (2048, 512), (2560, 256)]
    hq_stores = []
    prev4 = {}
    qo_t = {}

    def ph4a_tail(tp):
        par = tp % 2
        evs = []
        for kt in range(22):
            p2_ = kt % 2
            tdeps = [qo_t[tp], ld_id]
            if ("pt", p2_) in xblock:
                xblock[("pt", p2_)].sig = True
                tdeps.append(xblock[("pt", p2_)])
            t_tr = P.op("tensor", lambda e, k=kt, p=par, p_=p2_: e.transpose(
                ptb[p_], sHq[p][:, k * 128:(k + 1) * 128], s_id.ap()),
                tdeps, (), "c", f"p4.htr{tp}.{kt}")
            evdeps = [t_tr]
            if ("hq_store", par) in prev4 and kt == 0:
                evdeps.append(prev4[("hq_store", par)])
            t_ev = P.op("scalar", lambda e, k=kt, p=par, p_=p2_: e.copy(
                sHqT[p][:, k * 128:(k + 1) * 128], ptb[p_]),
                evdeps, (), "c", f"p4.hev{tp}.{kt}")
            xblock[("pt", p2_)] = t_ev
            evs.append(t_ev)
        prev4[("hqtr_last", par)] = evs[-1]
        t_hst = P.dma("sync", lambda e, t=tp, p=par: e.dma_start(
            out=b_hq.ap()[:, t * 128:(t + 1) * 128]
                .rearrange("(k p) c -> p k c", p=128),
            in_=sHqT[p].rearrange("p (k c) -> p k c", c=128)),
            evs, f"p4.hst{tp}", stream="p4.hst")
        hq_stores.append(t_hst)
        prev4[("hq_store", par)] = t_hst

    for tt in range(16):
        for ci4, (j0, ncw) in enumerate(NCC):
            pga, pgb = ([p_a, p_b], [p_c, p_d])[ci4 % 2]
            gdeps = [ld_wg, ph4_gate]
            if ("silu", ci4 % 2) in prev4:
                gdeps.append(prev4[("silu", ci4 % 2)])
            mmg = None
            for kt in range(8):
                mmg = P.op("tensor", lambda e, k=kt, t_=tt, j=j0, w=ncw, p_=pga: e.matmul(
                    p_.ap()[:, 0:w],
                    sXqfm[:, (k * 16 + t_) * 128:(k * 16 + t_) * 128 + 128],
                    sWqg[:, k * 2816 + j:k * 2816 + j + w],
                    start=(k == 0), stop=(k == 7)),
                    gdeps if kt == 0 else [], (), "c", f"p4.gmm{tt}.{j0}.{kt}")
            udeps = [ld_wu]
            if ("hmul", ci4 % 2) in prev4:
                udeps.append(prev4[("hmul", ci4 % 2)])
            mmu = None
            for kt in range(8):
                mmu = P.op("tensor", lambda e, k=kt, t_=tt, j=j0, w=ncw, p_=pgb: e.matmul(
                    p_.ap()[:, 0:w],
                    sXqfm[:, (k * 16 + t_) * 128:(k * 16 + t_) * 128 + 128],
                    sWqu[:, k * 2816 + j:k * 2816 + j + w],
                    start=(k == 0), stop=(k == 7)),
                    udeps if kt == 0 else [], (), "c", f"p4.umm{tt}.{j0}.{kt}")
            sdeps = [mmg]
            if ("hmul", ci4 % 2) in prev4:
                sdeps.append(prev4[("hmul", ci4 % 2)])
            t_sg = P.op("scalar", lambda e, w=ncw, p_=pga, sgo=(ci4 % 2) * 512: e.activation(
                out=sSg[:, sgo:sgo + w], in_=p_.ap()[:, 0:w], func=AF.Silu),
                sdeps, (), "c", f"p4.sg{tt}.{j0}")
            prev4[("silu", ci4 % 2)] = t_sg
            t_h = P.op("vector", lambda e, j=j0, w=ncw, p_=pgb, sgo=(ci4 % 2) * 512: e.tensor_tensor(
                out=sH[:, j:j + w], in0=sSg[:, sgo:sgo + w], in1=p_.ap()[:, 0:w],
                op=ALU.mult), [t_sg, mmu], (), "c", f"p4.h{tt}.{j0}")
            prev4[("hmul", ci4 % 2)] = t_h
            prev4[("h", j0)] = t_h
        q0, qo = quant_ops(P, None, sH, sHq[tt % 2], scr2)
        q0.deps += [prev4[("h", j0)] for (j0, _) in NCC]
        if ("hqtr_last", tt % 2) in prev4:
            qo.deps.append(prev4[("hqtr_last", tt % 2)])
        qo_t[tt] = qo
        if tt >= 1:
            ph4a_tail(tt - 1)
    ph4a_tail(15)

    P.emit(nc, bank)
    nc.all_engine_barrier()
    P5 = Prog()
    P = P5

    al = Al(ph4_base)
    sWqd = al.b(22528)
    sHqL = [al.b(2816), al.b(2816)]
    sRes2 = [al.f(1024), al.f(1024)]
    sOut = [al.f(1024), al.f(1024)]
    assert al.c <= ARN_COLS, al.c

    g5deps = [hq_stores[-1]]
    if not skip_wq:
        g5deps += [cc_wd]
    ph5_gate = P.op("sync", lambda e: e.nop(), g5deps, (), "c", "ph5.gate")
    ld_wd = None
    for rr in range(8):
        ld_wd = P.dma("sync", lambda e, r_=rr: e.dma_start(
            out=sWqd.rearrange("p (k c) -> p k c", k=22)[:, :, r_ * 128:(r_ + 1) * 128],
            in_=b_wqd_g.ap()[r_ * IFF:(r_ + 1) * IFF, :]
                .rearrange("(k p) c -> p k c", p=128)),
            [ph5_gate], f"p5.ldwd{rr}", stream="p5.ldwd")
    prev5 = {}
    out_stores = []
    for tt in range(16):
        par = tt % 2
        hdeps = [ph5_gate]
        if ("dmm_last", par) in prev5:
            hdeps.append(prev5[("dmm_last", par)])
        t_ldh = P.dma("sync", lambda e, t=tt, p=par: e.dma_start(
            out=sHqL[p].rearrange("p (k c) -> p k c", c=128),
            in_=b_hq.ap()[:, t * 128:(t + 1) * 128]
                .rearrange("(k p) c -> p k c", p=128)),
            hdeps, f"p5.ldh{tt}", stream=f"p5.ldh{tt % 2}")
        rdeps = [ph5_gate]
        if ("fin", par) in prev5:
            rdeps.append(prev5[("fin", par)])
        t_ldr2 = P.dma("sync", lambda e, t=tt, p=par: e.dma_start(
            out=sRes2[p], in_=b_res[t * 128:(t + 1) * 128, :]),
            rdeps, f"p5.ldr{tt}", stream=f"p5.ldr{tt % 2}")
        mdeps = [t_ldh, ld_wd]
        pbanks = [[p_e.ap()[:, 0:512], p_e.ap()[:, 512:1024]],
                  [p_a.ap()[:, 0:512], p_b.ap()[:, 0:512]]][par]
        mm = None
        mms = []
        for hh in range(2):
            if ("fin_h", par, hh) in prev5:
                mdeps_h = (mdeps + [prev5[("fin_h", par, hh)]]) if True else mdeps
            else:
                mdeps_h = mdeps
            for kt in range(22):
                mm = P.op("tensor", lambda e, k=kt, p=par, h=hh, pb=pbanks: e.matmul(
                    pb[h],
                    sHqL[p][:, k * 128:(k + 1) * 128],
                    sWqd[:, k * 1024 + h * 512:k * 1024 + (h + 1) * 512],
                    start=(k == 0), stop=(k == 21)),
                    mdeps_h if kt == 0 else [], (), "c",
                    f"p5.dmm{tt}.{hh}.{kt}")
            mms.append(mm)
        prev5[("dmm_last", par)] = mm
        t_fin = None
        for hh in range(2):
            fdeps = [mms[hh], t_ldr2]
            if ("out_st", par) in prev5 and hh == 0:
                fdeps.append(prev5[("out_st", par)])
            t_fin = P.op("vector", lambda e, p=par, h=hh, pb=pbanks: e.tensor_tensor(
                out=sOut[p][:, h * 512:(h + 1) * 512], in0=pb[h],
                in1=sRes2[p][:, h * 512:(h + 1) * 512], op=ALU.add),
                fdeps, (), "c", f"p5.fin{tt}.{hh}")
            prev5[("fin_h", par, hh)] = t_fin
        prev5[("fin", par)] = t_fin
        t_ost = P.dma("sync", lambda e, t=tt, p=par: e.dma_start(
            out=o_y[t * 128:(t + 1) * 128, :], in_=sOut[p]),
            [t_fin], f"p5.ost{tt}", stream=f"p5.ost{tt % 2}")
        prev5[("out_st", par)] = t_ost
        out_stores.append(t_ost)

    P.emit(nc, bank, final=list(taps.values()) + out_stores)
    return nc, taps
    return nc, taps


def prep_inputs(inputs, c):
    import ml_dtypes
    f32 = np.float32
    beta, j = c // 4, c % 4
    x = np.asarray(inputs["L0_attn_x"], f32)
    W_in = np.asarray(inputs["L0_attn_W_in"], f32)
    W_conv = np.asarray(inputs["L0_attn_W_conv"], f32)
    b_conv = np.asarray(inputs["L0_attn_b_conv"], f32)
    W_xproj = np.asarray(inputs["L0_attn_W_xproj"], f32)
    W_dt = np.asarray(inputs["L0_attn_W_dt"], f32)
    b_dt = np.asarray(inputs["L0_attn_b_dt"], f32)
    A_log = np.asarray(inputs["L0_attn_A_log"], f32)
    D = np.asarray(inputs["L0_attn_D"], f32)
    W_out = np.asarray(inputs["L0_attn_W_out"], f32)
    wg = np.asarray(inputs["L0_ffn_w_gate"], f32)
    wu = np.asarray(inputs["L0_ffn_w_up"], f32)
    wd = np.asarray(inputs["L0_ffn_w_down"], f32)

    sl = slice(j * DS, (j + 1) * DS)
    dconv = np.zeros((128, 4, 4, 128), f32)
    for m in range(4):
        for k in range(4):
            dconv[np.arange(128), m, k, np.arange(128)] = \
                W_conv[j * DS + m * 128:j * DS + (m + 1) * 128, 0, k]
    d = dict(
        x_t=np.ascontiguousarray(x[beta, (c % 4) * TOK:(c % 4 + 1) * TOK, :]),
        w_inT=np.ascontiguousarray(
            np.concatenate([W_in[sl], W_in[DI + j * DS:DI + (j + 1) * DS]], 0).T),
        dconv=np.ascontiguousarray(dconv.reshape(128, 16 * 128)),
        bconv_t=np.ascontiguousarray(b_conv[sl].reshape(4, 128).T),
        w_xpT=np.ascontiguousarray(W_xproj[:, sl].T),
        w_dtT=np.ascontiguousarray(W_dt[sl, :].T),
        bdt_t=np.ascontiguousarray(b_dt[sl].reshape(4, 128).T),
        alog_t=np.ascontiguousarray(
            A_log[sl].reshape(4, 128, NST).transpose(1, 0, 2).reshape(128, 4 * NST)),
        d_t=np.ascontiguousarray(D[sl].reshape(4, 128).T),
        w_outT=np.ascontiguousarray(W_out[:, sl].T),
        wg_sh=np.ascontiguousarray(wg[c * IFS:(c + 1) * IFS, :]),
        wu_sh=np.ascontiguousarray(wu[c * IFS:(c + 1) * IFS, :]),
        wd_sh=np.ascontiguousarray(wd[c * WDS:(c + 1) * WDS, :]),
        ident=np.eye(128, dtype=ml_dtypes.bfloat16),
        identf=np.eye(128, dtype=np.float32),
    )
    return d


_NC_CACHE = {}


def kernel(**inputs):
    key = "k"
    if key not in _NC_CACHE:
        _NC_CACHE[key] = build(upto=4)
    nc, _ = _NC_CACHE[key]
    in_maps = [prep_inputs(inputs, c) for c in range(8)]
    res = run_bass_kernel_spmd(nc, in_maps, core_ids=list(range(8)))
    out = np.empty((B, L, DM), np.float32)
    for c in range(8):
        beta, j = c // 4, c % 4
        out[beta, j * TOK:(j + 1) * TOK, :] = res.results[c]["o_y"]
    return out
